# version 22
# baseline (speedup 1.0000x reference)
import numpy as np
import ml_dtypes
from contextlib import ExitStack

import concourse.bass as bass
import concourse.mybir as mybir
from concourse.bass_utils import run_bass_kernel_spmd

F32 = mybir.dt.float32
BF16 = mybir.dt.bfloat16
F16 = mybir.dt.float16
AF = mybir.ActivationFunctionType
ALU = mybir.AluOpType
AX = mybir.AxisListType

B, T, D = 4, 4096, 2048
NCORES = 8
OWN = 2048
NT_OWN = 16
NT_FULL = 32
FC = 16
GD = 512
NFF = 8192
INV_SQRT_HS = 0.08838834764831845
NEG = -1.0e30


class Ctx:

    def __init__(self, nc, stack):
        self.nc = nc
        self.stack = stack
        self.sem = {}
        self.cnt = {}
        self.lastw = {}
        for e in ("pe", "dve", "act", "gpc"):
            self._new(e)

    def _new(self, name):
        self.sem[name] = self.stack.enter_context(nc_semaphore(self.nc, name))
        self.cnt[name] = 0

    def ensure(self, name):
        if name not in self.sem:
            self._new(name)
        return self.sem[name]

    def mark(self, e, inst, k=1):
        inst.then_inc(self.sem[e], k)
        self.cnt[e] += k
        return self.cnt[e]

    def dmark(self, name, inst):
        self.ensure(name)
        return self.mark(name, inst, 16)

    def wait(self, wname, handle, prod, v):
        if v is None or v <= 0 or prod not in self.sem:
            return
        assert v <= self.cnt[prod], f"future wait {prod} {v} > {self.cnt[prod]}"
        if v > self.lastw.get((wname, prod), 0):
            handle.wait_ge(self.sem[prod], v)
            self.lastw[(wname, prod)] = v

    def waitt(self, wname, handle, tup):
        if tup is not None:
            self.wait(wname, handle, tup[0], tup[1])

    def snap(self):
        return dict(self.cnt)

    BARRIER_SKIP = ("gw0", "gw1")

    def barrier(self, wname, handle, snap):
        for e, v in snap.items():
            if e in self.BARRIER_SKIP:
                continue
            self.wait(wname, handle, e, v)


def nc_semaphore(nc, name):
    return nc.semaphore("sem_" + name)


def build_program(debug=False):
    nc = bass.Bass("TRN2", target_bir_lowering=False)

    xf = nc.dram_tensor("x_full", [T, D], F32, kind="ExternalInput").ap()
    xo = nc.dram_tensor("x_own", [OWN, D], F32, kind="ExternalInput").ap()
    cmask = nc.dram_tensor("cmask", [128, 256], F32, kind="ExternalInput").ap()
    wqt = nc.dram_tensor("wqt", [D, D], F32, kind="ExternalInput").ap()
    bqv = nc.dram_tensor("bq", [1, D], F32, kind="ExternalInput").ap()
    wkt = nc.dram_tensor("wkt", [D, GD], F32, kind="ExternalInput").ap()
    bkt = nc.dram_tensor("bkt", [128, 4], F32, kind="ExternalInput").ap()
    wvt = nc.dram_tensor("wvt", [D, GD], F32, kind="ExternalInput").ap()
    bvr = nc.dram_tensor("bv", [1, GD], F32, kind="ExternalInput").ap()
    pj = nc.dram_tensor("proj", [D, D], F32, kind="ExternalInput").ap()
    w0t = nc.dram_tensor("w0t", [D, NFF], F32, kind="ExternalInput").ap()
    b0 = nc.dram_tensor("b0", [128, 64], F32, kind="ExternalInput").ap()
    w1t = nc.dram_tensor("w1t", [NFF, D], F32, kind="ExternalInput").ap()
    b1 = nc.dram_tensor("b1", [1, D], F32, kind="ExternalInput").ap()
    out = nc.dram_tensor("out_own", [OWN, D], F32, kind="ExternalOutput").ap()

    ikind = "ExternalOutput" if debug else "Internal"
    wqstb = nc.dram_tensor("wqstb", [D, GD], BF16, kind=ikind).ap()
    projb = nc.dram_tensor("projb", [GD, D], BF16, kind=ikind).ap()
    bqd = nc.dram_tensor("bqd", [1, GD], F32, kind="Internal").ap()
    w0tb = nc.dram_tensor("w0tb", [D, NFF], BF16, kind="Internal").ap()
    w1tb = nc.dram_tensor("w1tb", [NFF, D], BF16, kind="Internal").ap()
    xnt_d = nc.dram_tensor("xnt_d", [4, 128, FC * 512], BF16, kind=ikind).ap()
    att16 = nc.dram_tensor("att16", [OWN, D], F16, kind=ikind).ap()
    if debug:
        kdbg = nc.dram_tensor("kdbg", [128, 4 * T], BF16, kind="ExternalOutput").ap()
        vdbg = nc.dram_tensor("vdbg", [128, 32 * 512], BF16, kind="ExternalOutput").ap()
        qdbg = nc.dram_tensor("qdbg", [128, 4 * OWN], BF16, kind="ExternalOutput").ap()
        ddbg = nc.dram_tensor("ddbg", [128, 64 * 512], BF16, kind="ExternalOutput").ap()
        edbg = nc.dram_tensor("edbg", [OWN, D], F16, kind="ExternalOutput").ap()

    with ExitStack() as outer:
        cx = Ctx(nc, outer)
        blk = outer.enter_context(nc.Block())
        W = cx.wait
        WT = cx.waitt
        M = cx.mark
        marks = {}

        def sbt(st, n, s, d):
            return st.enter_context(nc.sbuf_tensor(n, s, d))

        identb = sbt(outer, "identb", [128, 128], BF16)
        ident32 = sbt(outer, "ident32", [64, 64], F32)
        ones_r = sbt(outer, "ones_r", [1, 128], BF16)
        bv_r = sbt(outer, "bv_r", [1, GD], BF16)
        bkt_sb = sbt(outer, "bkt_sb", [128, 4], F32)
        bqt_sb = sbt(outer, "bqt_sb", [128, 4], F32)
        b0t = sbt(outer, "b0t", [128, 64], F32)
        cm_sb = sbt(outer, "cm_sb", [128, 256], F32)
        stats = sbt(outer, "stats", [128, 2, 24], F32)
        neghalf = sbt(outer, "neghalf", [128, 1], F32)
        smax = sbt(outer, "smax", [128, 1], F32)
        pmax = sbt(outer, "pmax", [128, 4], F32)
        ssum = sbt(outer, "ssum", [128, 1], F32)
        srs = sbt(outer, "srs", [128, 1], F32)
        esum = sbt(outer, "esum", [128, 4], F32)
        accw = sbt(outer, "accw", [128, D], F32)
        bqf = sbt(outer, "bqf", [1, GD], F32)
        bq4_sb = sbt(outer, "bq4_sb", [4, 128], F32)

        if True:
            def g_p0a(gp):
                gp.memset(ones_r[:], 1.0)
                gp.memset(neghalf[:], -0.5)
                m = M("gpc", gp.memset(identb[:], 0.0))
                gp.wait_ge(cx.sem["gpc"], m)
                inst = gp.affine_select(
                    out=identb[:], in_=identb[:], compare_op=ALU.not_equal,
                    fill=1.0, base=0, pattern=[[-1, 128]], channel_multiplier=1)
                m = M("gpc", gp.memset(ident32[:], 0.0))
                gp.wait_ge(cx.sem["gpc"], m)
                inst = gp.affine_select(
                    out=ident32[:], in_=ident32[:], compare_op=ALU.not_equal,
                    fill=1.0, base=0, pattern=[[-1, 64]], channel_multiplier=1)
                marks["p0_consts"] = M("gpc", inst)
                for ap_out, ap_in in ((bv_r[:], bvr), (bkt_sb[:], bkt),
                                      (b0t[:], b0), (cm_sb[:], cmask)):
                    cx.dmark("gin", gp.dma_start(out=ap_out, in_=ap_in))
                marks["p0_loads"] = cx.cnt["gin"]
            blk.gpsimd(g_p0a)

        def g_p0b(gp):
            for r in range(4):
                op = ALU.bypass if r == 0 else ALU.add
                cx.dmark("gbq", gp.dma_start(
                    out=bqf[:], in_=bqv[0:1, r * GD:(r + 1) * GD], accum_op=op))
                gp.wait_ge(cx.sem["gbq"], cx.cnt["gbq"])
            cx.dmark("gbq", gp.dma_start(out=bqd, in_=bqf[:]))
            marks["gbq_fin"] = cx.cnt["gbq"]
            for base in range(0, FC, 4):
                for r in range(4):
                    for j in range(4):
                        fc = base + j
                        sl = f"gac{j}"
                        cx.ensure(sl)
                        if cx.cnt[sl] > 0:
                            gp.wait_ge(cx.sem[sl], cx.cnt[sl])
                        op = ALU.bypass if r == 0 else ALU.add
                        cx.dmark(sl, gp.dma_start(
                            out=accw[:, j * GD:(j + 1) * GD],
                            in_=wqt[fc * 128:(fc + 1) * 128,
                                    r * GD:(r + 1) * GD],
                            accum_op=op))
                for j in range(4):
                    fc = base + j
                    sl = f"gac{j}"
                    gp.wait_ge(cx.sem[sl], cx.cnt[sl])
                    cx.dmark(sl, gp.dma_start(
                        out=wqstb[fc * 128:(fc + 1) * 128, :],
                        in_=accw[:, j * GD:(j + 1) * GD]))
            for j in range(4):
                marks[f"gac{j}_fin"] = cx.cnt[f"gac{j}"]

        def g_pjfold(gp):
            for j in range(4):
                gp.wait_ge(cx.sem[f"gac{j}"], cx.cnt[f"gac{j}"])
            cx.ensure("gpj0")
            cx.ensure("gpj1")
            for g in range(4):
                for r in range(4):
                    op = ALU.bypass if r == 0 else ALU.add
                    for h in range(2):
                        sl = f"gpj{h}"
                        if cx.cnt[sl] > 0:
                            gp.wait_ge(cx.sem[sl], cx.cnt[sl])
                        cx.dmark(sl, gp.dma_start(
                            out=accw[:, h * 1024:(h + 1) * 1024],
                            in_=pj[(g + 4 * r) * 128:(g + 4 * r + 1) * 128,
                                   h * 1024:(h + 1) * 1024],
                            accum_op=op))
                for h in range(2):
                    sl = f"gpj{h}"
                    gp.wait_ge(cx.sem[sl], cx.cnt[sl])
                    cx.dmark(sl, gp.dma_start(
                        out=projb[g * 128:(g + 1) * 128,
                                  h * 1024:(h + 1) * 1024],
                        in_=accw[:, h * 1024:(h + 1) * 1024]))
            marks["gpj_fin0"] = cx.cnt["gpj0"]
            marks["gpj_fin1"] = cx.cnt["gpj1"]

        with ExitStack() as span:
            k_fm = sbt(span, "k_fm", [128, 4, T], BF16)
            v_tm = sbt(span, "v_tm", [128, 32, 512], BF16)
            q_fm = sbt(span, "q_fm", [128, 4, OWN], BF16)

            with ExitStack() as pb:
                x_sb = sbt(pb, "x_sb", [128, 5, D], F32)
                xn_sem = {"f": "dve", "q": "dve"}
                xn_sb = sbt(pb, "xn_sb", [128, 4, D], BF16)
                xng_sb = sbt(pb, "xng", [128, FC, 512], BF16)
                sq_scr = sbt(pb, "sq_scr", [128, D], BF16)

                def emit_loads(g4, src, tag, trange, ebar=None):
                    base = 4 * g4

                    def emit_on(tlist, bfn, wname):
                        if not tlist:
                            return

                        def _ld(h):
                            if base == 0 and ebar is not None:
                                cx.barrier(wname, h, ebar)
                            for t in tlist:
                                i = base + t
                                sl = i % 5
                                W(wname, h, "dve", marks.get((tag, "red", i - 5)))
                                W(wname, h, "act", marks.get((tag, "sq", i - 5)))
                                W(wname, h, xn_sem[tag],
                                  marks.get((tag, "xnrun", (i - 5) // 4))
                                  if i >= 5 else None)
                                inst = h.dma_start(
                                    out=x_sb[:, sl, :],
                                    in_=src[i * 128:(i + 1) * 128, :])
                                marks[(tag, "xload", i)] = cx.dmark(f"sx{sl}", inst)
                        bfn(_ld)
                    emit_on([t for t in trange if t % 2 == 0], blk.sync, "sq")
                    emit_on([t for t in trange if t % 2 == 1], blk.scalar, "act")

                def emit_sqrun(g4, tag, ebar=None):
                    gb = g4 % 2
                    stg = stats[:, gb, :]
                    base = 4 * g4

                    def a_sq(sc):
                        if base == 0 and ebar is not None:
                            cx.barrier("act", sc, ebar)
                        if base == 0 and tag == "f":
                            W("act", sc, "gpc", marks["p0_consts"])
                        W("act", sc, "dve", marks.get((tag, "stats", g4 - 2)))
                        for t in range(4):
                            i = base + t
                            W("act", sc, f"sx{i % 5}", marks[(tag, "xload", i)])
                            if t > 0:
                                W("act", sc, "act", marks[(tag, "sq", i - 1)])
                            marks[(tag, "sq", i)] = M("act", sc.activation(
                                sq_scr[:], x_sb[:, i % 5, :], AF.Square,
                                accum_out=stg[:, 4 + t:5 + t]))
                        marks[(tag, "sqrun", g4)] = marks[(tag, "sq", base + 3)]
                    blk.scalar(a_sq)

                def emit_vpre(g4, tag, ebar=None):
                    gb = g4 % 2
                    stg = stats[:, gb, :]
                    base = 4 * g4

                    def v_pre(ve):
                        if base == 0 and ebar is not None:
                            cx.barrier("dve", ve, ebar)
                        W("dve", ve, xn_sem[tag],
                          marks.get((tag, "xnrun", g4 - 2)))
                        for t in range(4):
                            i = base + t
                            W("dve", ve, f"sx{i % 5}", marks[(tag, "xload", i)])
                            marks[(tag, "red", i)] = M("dve", ve.reduce_sum(
                                stg[:, t:t + 1], x_sb[:, i % 5, :], axis=AX.X))
                        W("dve", ve, "dve", marks[(tag, "red", base + 3)])
                        W("dve", ve, "act", marks[(tag, "sqrun", g4)])
                        M("dve", ve.tensor_scalar_mul(
                            stg[:, 8:12], stg[:, 0:4], -1.0 / D))
                        m = M("dve", ve.tensor_scalar(
                            stg[:, 12:16], stg[:, 4:8], 1.0 / D, 1.0e-5,
                            ALU.mult, ALU.add))
                        W("dve", ve, "dve", m)
                        m = M("dve", ve.tensor_mul(
                            stg[:, 16:20], stg[:, 8:12], stg[:, 8:12]))
                        W("dve", ve, "dve", m)
                        m = M("dve", ve.tensor_sub(
                            stg[:, 12:16], stg[:, 12:16], stg[:, 16:20]))
                        W("dve", ve, "dve", m)
                        marks[(tag, "stats", g4)] = M("dve", ve.reciprocal(
                            stg[:, 16:20], stg[:, 12:16]))
                    blk.vector(v_pre)

                def emit_arun(g4, src, tag, ngroups, ebar=None):
                    gb = g4 % 2
                    stg = stats[:, gb, :]
                    base = 4 * g4

                    def a_run(sc):
                        W("act", sc, "dve", marks[(tag, "stats", g4)])
                        W("act", sc, "dve", marks.get((tag, "xnrun", g4 - 2)))
                        marks[(tag, "lnrun", g4)] = M("act", sc.activation(
                            stg[:, 20:24], stg[:, 16:20], AF.Sqrt))
                    blk.scalar(a_run)


                def emit_tiles(g4, tag, trp, vps, with_v, nxt=None):
                    gb = g4 % 2
                    stg = stats[:, gb, :]
                    base = 4 * g4
                    for t in range(4):
                        i = base + t

                        def v_xn(ve, t=t, i=i):
                            W("dve", ve, "act", marks[(tag, "lnrun", g4)])
                            W("dve", ve, f"sx{i % 5}", marks[(tag, "xload", i)])
                            W("dve", ve, "pe", marks.get((tag, "tr", i - 4)))
                            inst = ve.tensor_scalar(
                                xn_sb[:, t, :], x_sb[:, i % 5, :],
                                stg[:, 8 + t:9 + t], stg[:, 20 + t:21 + t],
                                ALU.add, ALU.mult)
                            marks[(tag, "xn", i)] = M("dve", inst)
                            if t == 3:
                                marks[(tag, "xnrun", g4)] = marks[(tag, "xn", i)]
                        blk.vector(v_xn)

                        if nxt is not None and t < 3:
                            t2 = t + 1
                            i2 = base + 4 + t2
                            sl2 = i2 % 5
                            bfn = blk.scalar if t2 % 2 == 1 else blk.sync
                            wname = "act" if t2 % 2 == 1 else "sq"

                            def _ld(h, t2=t2, i2=i2, sl2=sl2, wname=wname):
                                W(wname, h, "dve", marks[(tag, "xn", i2 - 5)])
                                W(wname, h, "act", marks[(tag, "sq", i2 - 5)])
                                inst = h.dma_start(
                                    out=x_sb[:, sl2, :],
                                    in_=nxt[i2 * 128:(i2 + 1) * 128, :])
                                marks[(tag, "xload", i2)] = cx.dmark(
                                    f"sx{sl2}", inst)
                            bfn(_ld)

                        def t_tr(te, t=t, i=i):
                            if i == 0:
                                W("pe", te, "gpc", marks["p0_consts"])
                            W("pe", te, xn_sem[tag], marks[(tag, "xn", i)])
                            W("pe", te, "dve", marks.get((tag, "trev", i - 1)))
                            for half in range(2):
                                for j in range(8):
                                    c = half * 8 + j
                                    inst = te.matmul(
                                        trp[:, half, j],
                                        xn_sb[:, t, c * 128:(c + 1) * 128],
                                        identb[:], start=(j == 0), stop=(j == 7),
                                        is_transpose=True)
                                marks[(tag, "trh", i, half)] = M("pe", inst)
                            marks[(tag, "tr", i)] = marks[(tag, "trh", i, 1)]
                        blk.tensor(t_tr)

                        def v_trev(ve, t=t, i=i):
                            W("dve", ve, "pe", marks[(tag, "trh", i, 0)])
                            if t == 0:
                                W("dve", ve, "pe", marks.get((tag, "kmm", g4 - 1)))
                                if tag == "q":
                                    W("dve", ve, "gxn0",
                                      marks.get(("q", "xnship", g4 - 1)))
                            src2 = trp.ap().rearrange("p a b c -> p (a b) c")
                            dst = xng_sb[:, :, t * 128:(t + 1) * 128]
                            M("dve", ve.tensor_copy(dst[:, 0:8, :], src2[:, 0:8, :]))
                            W("dve", ve, "pe", marks[(tag, "trh", i, 1)])
                            marks[(tag, "trev", i)] = M("dve", ve.tensor_copy(
                                dst[:, 8:16, :], src2[:, 8:16, :]))
                        blk.vector(v_trev)

                        if with_v:
                            def t_vmm(te, t=t, i=i):
                                W("pe", te, "dve", marks[(tag, "trev", i)])
                                if i == 0:
                                    W("pe", te, "swk", marks["wkvt_v_done"])
                                    W("pe", te, "gin", marks["p0_loads"])
                                W("pe", te, "act", marks.get((tag, "vev", i - 2)))
                                ps = vps[:, i % 2, :]
                                for fc in range(FC):
                                    te.matmul(
                                        ps, xng_sb[:, fc, t * 128:(t + 1) * 128],
                                        wkvt[:, fc, 512:1024],
                                        start=(fc == 0), stop=False)
                                inst = te.matmul(ps, ones_r[:], bv_r[:],
                                                 start=False, stop=True)
                                marks[(tag, "vmm", i)] = M("pe", inst)
                            blk.tensor(t_vmm)

                            def a_vev(sc, t=t, i=i):
                                W("act", sc, "pe", marks[(tag, "vmm", i)])
                                marks[(tag, "vev", i)] = M("act", sc.copy(
                                    v_tm[:, i, :], vps[:, i % 2, :]))
                            blk.scalar(a_vev)

                def emit_kqmm(g4, tag, kps, wt_sb, dst_fm, bias_sb):
                    def t_kmm(te):
                        W("pe", te, "dve", marks[(tag, "trev", 4 * g4 + 3)])
                        W("pe", te, "dve", marks.get((tag, "kev", g4 - 1)))
                        if g4 == 0 and tag == "f":
                            W("pe", te, "swk", marks["wkvt_done"])
                        if g4 == 0 and tag == "q":
                            W("pe", te, "swq", marks["wqt_done"])
                        for kc in range(4):
                            for fc in range(FC):
                                inst = te.matmul(
                                    kps[:, kc, :],
                                    wt_sb[:, fc, kc * 128:(kc + 1) * 128],
                                    xng_sb[:, fc, :],
                                    start=(fc == 0), stop=(fc == FC - 1))
                            marks[(tag, "kmmc", g4, kc)] = M("pe", inst)
                        marks[(tag, "kmm", g4)] = marks[(tag, "kmmc", g4, 3)]
                    blk.tensor(t_kmm)

                    def v_kev(ve):
                        if g4 == 0 and tag == "f":
                            W("dve", ve, "gin", marks["p0_loads"])
                        if g4 == 0 and tag == "q":
                            W("dve", ve, "dve", marks["bqt_done"])
                        for kc in range(4):
                            W("dve", ve, "pe", marks[(tag, "kmmc", g4, kc)])
                            inst = ve.tensor_scalar(
                                dst_fm[:, kc, g4 * 512:(g4 + 1) * 512],
                                kps[:, kc, :], bias_sb[:, kc:kc + 1], None,
                                ALU.add)
                            M("dve", inst)
                        marks[(tag, "kev", g4)] = cx.cnt["dve"]
                    blk.vector(v_kev)

                with ExitStack() as pkv:
                    wkvt = sbt(pkv, "wkvt", [128, FC, 1024], BF16)
                    kps = pkv.enter_context(nc.psum_tensor("kps", [128, 4, 512], F32))
                    vps = pkv.enter_context(nc.psum_tensor("vps", [128, 2, 512], F32))
                    trp = pkv.enter_context(nc.psum_tensor("trp", [128, 2, 8, 128], BF16))

                    emit_loads(0, xf, "f", range(4))

                    def g_wkv(gp):
                        for fc in range(FC):
                            cx.dmark("swk", gp.dma_start(
                                out=wkvt[:, fc, 512:1024],
                                in_=wvt[fc * 128:(fc + 1) * 128, :]))
                        marks["wkvt_v_done"] = cx.cnt["swk"]
                        for fc in range(FC):
                            cx.dmark("swk", gp.dma_start(
                                out=wkvt[:, fc, 0:512],
                                in_=wkt[fc * 128:(fc + 1) * 128, :]))
                        marks["wkvt_done"] = cx.cnt["swk"]
                    blk.gpsimd(g_wkv)
                    blk.gpsimd(g_p0b)

                    emit_sqrun(0, "f")
                    for g4 in range(8):
                        emit_vpre(g4, "f")
                        if g4 + 1 < 8:
                            emit_loads(g4 + 1, xf, "f", range(1))
                        emit_arun(g4, xf, "f", 8)
                        emit_tiles(g4, "f", trp, vps, with_v=True,
                                   nxt=xf if g4 + 1 < 8 else None)
                        if g4 + 1 < 8:
                            emit_sqrun(g4 + 1, "f")
                        emit_kqmm(g4, "f", kps, wkvt, k_fm, bkt_sb)

                blk.gpsimd(g_pjfold)

                kv_end = cx.snap()

                with ExitStack() as pq:
                    wqtb_sb = sbt(pq, "wqt_sb", [128, FC, 512], BF16)
                    qps = pq.enter_context(nc.psum_tensor("qps", [128, 4, 512], F32))
                    trpq = pq.enter_context(nc.psum_tensor("trpq", [128, 2, 8, 128], BF16))
                    bqps = pq.enter_context(nc.psum_tensor("bqps", [128, 4], F32))

                    def s_wq(sy):
                        cx.barrier("sq", sy, kv_end)
                        for j in range(4):
                            W("sq", sy, f"gac{j}", marks[f"gac{j}_fin"])
                        for fc in range(FC):
                            cx.dmark("swq", sy.dma_start(
                                out=wqtb_sb[:, fc, :],
                                in_=wqstb[fc * 128:(fc + 1) * 128, :]))
                        marks["wqt_done"] = cx.cnt["swq"]
                        W("sq", sy, "gbq", marks["gbq_fin"])
                        inst = sy.dma_start(
                            out=bq4_sb[:],
                            in_=bqd.rearrange("1 (a b) -> a b", a=4))
                        marks["bq4"] = cx.dmark("sbq", inst)
                    blk.sync(s_wq)

                    def t_bq(te):
                        cx.barrier("pe", te, kv_end)
                        W("pe", te, "sbq", marks["bq4"])
                        marks["bqtr"] = M("pe", te.transpose(
                            bqps[:], bq4_sb[:], ident32[0:4, 0:4]))
                    blk.tensor(t_bq)

                    def v_bq(ve):
                        cx.barrier("dve", ve, kv_end)
                        W("dve", ve, "pe", marks["bqtr"])
                        marks["bqt_done"] = M("dve", ve.tensor_copy(
                            bqt_sb[:], bqps[:]))
                    blk.vector(v_bq)

                    emit_loads(0, xo, "q", range(4), ebar=kv_end)
                    emit_sqrun(0, "q", ebar=kv_end)
                    for g4 in range(4):
                        emit_vpre(g4, "q", ebar=kv_end)
                        if g4 + 1 < 4:
                            emit_loads(g4 + 1, xo, "q", range(1))
                        emit_arun(g4, xo, "q", 4, ebar=kv_end)
                        emit_tiles(g4, "q", trpq, None, with_v=False,
                                   nxt=xo if g4 + 1 < 4 else None)
                        if g4 + 1 < 4:
                            emit_sqrun(g4 + 1, "q")
                        emit_kqmm(g4, "q", qps, wqtb_sb, q_fm, bqt_sb)

                        def s_xnship(sy, g4=g4):
                            W("sq", sy, "pe", marks[("q", "kmm", g4)])
                            inst = sy.dma_start(
                                out=xnt_d[g4],
                                in_=xng_sb.ap().rearrange("p a b -> p (a b)"))
                            marks[("q", "xnship", g4)] = cx.dmark("gxn0", inst)
                        blk.sync(s_xnship)

                def g_w0cast(gp):
                    for c in range(4):
                        cx.dmark("gw0", gp.dma_start(
                            out=w0tb[c * 512:(c + 1) * 512, :],
                            in_=w0t[c * 512:(c + 1) * 512, :]))
                    marks["gw0_fin"] = cx.cnt["gw0"]
                blk.gpsimd(g_w0cast)

                def g_w1cast(gp):
                    for c in range(8):
                        cx.dmark("gw1", gp.dma_start(
                            out=w1tb[c * 1024:(c + 1) * 1024, :],
                            in_=w1t[c * 1024:(c + 1) * 1024, :]))
                        marks[("w1b", c)] = cx.cnt["gw1"]
                blk.gpsimd(g_w1cast)

                q_done = cx.snap()

            if debug:
                if True:
                    def g_dbg(gp):
                        W("gq", gp, "act", cx.cnt["act"])
                        W("gq", gp, "dve", cx.cnt["dve"])
                        cx.dmark("gdbg", gp.dma_start(
                            out=kdbg, in_=k_fm.ap().rearrange("p a b -> p (a b)")))
                        cx.dmark("gdbg", gp.dma_start(
                            out=vdbg, in_=v_tm.ap().rearrange("p a b -> p (a b)")))
                        cx.dmark("gdbg", gp.dma_start(
                            out=qdbg, in_=q_fm.ap().rearrange("p a b -> p (a b)")))
                    blk.gpsimd(g_dbg)

            pass_end = cx.snap()

            with ExitStack() as pa:
                sc_sb = sbt(pa, "sc_sb", [128, T], F32)
                p_sb = sbt(pa, "p_sb", [128, T], BF16)
                pt_sb = sbt(pa, "pt_sb", [128, 2, 8, 128], BF16)
                z_sb = sbt(pa, "z_sb", [128, 512], BF16)
                zt_sb = sbt(pa, "zt_sb", [128, 4, 128], BF16)
                af_sb = sbt(pa, "af_sb", [128, 2, D], F16)
                projsum = sbt(pa, "projsum", [128, 4, D], BF16)
                scp = pa.enter_context(nc.psum_tensor("scp", [128, 2, 1024], F32))
                zps = pa.enter_context(nc.psum_tensor("zps", [128, 512], F32))
                ptp = pa.enter_context(nc.psum_tensor("ptp", [128, 2, 8, 128], BF16))

                rel = {"scp0": None, "scp1": None, "zps_free": None,
                       "zsb_free": None, "ptp0": None, "ptp1": None,
                       "ptsb0": None, "ptsb1": None, "zt_free": None}

                def nchunks(i):
                    return (256 * (i + 1) + 1023) // 1024

                if True:
                    def s_pj(sy):
                        cx.barrier("sq", sy, pass_end)
                        W("sq", sy, "gpj0", marks["gpj_fin0"])
                        W("sq", sy, "gpj1", marks["gpj_fin1"])
                        for g in range(4):
                            inst = sy.dma_start(out=projsum[:, g, :],
                                                in_=projb[g * 128:(g + 1) * 128, :])
                            cx.dmark("spj", inst)
                        marks["projsum_done"] = cx.cnt["spj"]
                    blk.sync(s_pj)

                    def emit_scores(i):
                        Ki = 256 * (i + 1)
                        nch = nchunks(i)
                        for c in range(nch):
                            wdt = min(1024, Ki - 1024 * c)

                            def t_sc(te, i=i, c=c, wdt=wdt):
                                if i == 0 and c == 0:
                                    cx.barrier("pe", te, pass_end)
                                WT("pe", te, rel[f"scp{c % 2}"])
                                ps = scp[:, c % 2, :]
                                for qc in range(4):
                                    inst = te.matmul(
                                        ps[:, 0:min(512, wdt)],
                                        q_fm[:, qc, i * 128:(i + 1) * 128],
                                        k_fm[:, qc, 1024 * c:1024 * c + min(512, wdt)],
                                        start=(qc == 0), stop=(qc == 3))
                                    if wdt > 512:
                                        inst = te.matmul(
                                            ps[:, 512:wdt],
                                            q_fm[:, qc, i * 128:(i + 1) * 128],
                                            k_fm[:, qc,
                                                 1024 * c + 512:1024 * c + wdt],
                                            start=(qc == 0), stop=(qc == 3))
                                marks[("a", "sc", i, c)] = M("pe", inst)
                            blk.tensor(t_sc)

                            def a_scev(sc, i=i, c=c, wdt=wdt):
                                if i == 0 and c == 0:
                                    cx.barrier("act", sc, pass_end)
                                W("act", sc, "pe", marks[("a", "sc", i, c)])
                                m = M("act", sc.copy(
                                    sc_sb[:, 1024 * c:1024 * c + wdt],
                                    scp[:, c % 2, 0:wdt]))
                                marks[("a", "scev", i, c)] = m
                                rel[f"scp{c % 2}"] = ("act", m)
                            blk.scalar(a_scev)

                            def v_scev(ve, i=i, c=c, wdt=wdt, nch=nch, Ki=Ki):
                                if i == 0 and c == 0:
                                    cx.barrier("dve", ve, pass_end)
                                W("dve", ve, "act", marks[("a", "scev", i, c)])
                                if c == nch - 1:
                                    m2 = M("dve", ve.tensor_add(
                                        sc_sb[:, Ki - 256:Ki],
                                        sc_sb[:, Ki - 256:Ki], cm_sb[:]))
                                    W("dve", ve, "dve", m2)
                                marks[("a", "pm", i, c)] = M("dve", ve.reduce_max(
                                    pmax[:, c:c + 1],
                                    sc_sb[:, 1024 * c:1024 * c + wdt], axis=AX.X))
                            blk.vector(v_scev)

                    def emit_softmax(i):
                        nch = nchunks(i)
                        Ki = 256 * (i + 1)

                        def v_max(ve, i=i, nch=nch):
                            W("dve", ve, "dve", marks[("a", "pm", i, nch - 1)])
                            W("dve", ve, "act", marks.get(("a", "expl", i - 1)))
                            marks[("a", "max", i)] = M("dve", ve.reduce_max(
                                smax[:, 0:1], pmax[:, 0:nch], axis=AX.X, negate=True))
                        blk.vector(v_max)

                        def a_exp(sc, i=i, Ki=Ki):
                            if i >= 1:
                                W("act", sc, "pe", marks[("a", "op", i - 1)])
                                W("act", sc, f"gaf{(i - 1) % 2}",
                                  marks.get(("a", "afs", i - 3)))
                                sc.copy(af_sb[:, (i - 1) % 2, 0:1024], scp[:, 0, :])
                                inst = sc.copy(af_sb[:, (i - 1) % 2, 1024:2048],
                                               scp[:, 1, :])
                                m = M("act", inst)
                                marks[("a", "af", i - 1)] = m
                                rel["scp0"] = ("act", m)
                                rel["scp1"] = ("act", m)
                            W("act", sc, "dve", marks[("a", "max", i)])
                            W("act", sc, "dve", marks.get(("a", "rs", i - 1)))
                            nch2 = nchunks(i)
                            for c in range(nch2):
                                wdt = min(1024, Ki - 1024 * c)
                                marks[("a", "exp", i, c)] = M("act", sc.activation(
                                    p_sb[:, 1024 * c:1024 * c + wdt],
                                    sc_sb[:, 1024 * c:1024 * c + wdt], AF.Exp,
                                    bias=smax[:, 0:1],
                                    accum_out=esum[:, c:c + 1]))
                            marks[("a", "expl", i)] = marks[("a", "exp", i, nch2 - 1)]
                        blk.scalar(a_exp)

                        def g_afs_prev(sy, i=i):
                            if i >= 1:
                                W("sq", sy, "act", marks[("a", "af", i - 1)])
                                inst = sy.dma_start(
                                    out=att16[(i - 1) * 128:i * 128, :],
                                    in_=af_sb[:, (i - 1) % 2, :])
                                marks[("a", "afs", i - 1)] = cx.dmark(
                                    f"gaf{(i - 1) % 2}", inst)
                        blk.sync(g_afs_prev)

                        def v_rs(ve, i=i, nch=nch):
                            W("dve", ve, "act", marks[("a", "expl", i)])
                            m = M("dve", ve.reduce_sum(
                                ssum[:, 0:1], esum[:, 0:nch], axis=AX.X))
                            W("dve", ve, "dve", m)
                            marks[("a", "rs", i)] = M("dve", ve.reciprocal(
                                srs[:, 0:1], ssum[:, 0:1]))
                        blk.vector(v_rs)

                    def emit_pv(i):
                        nkb = 2 * i + 2
                        npk = (nkb + 7) // 8
                        for p in range(npk):
                            j0, j1 = p * 8, min(nkb, p * 8 + 8)

                            def t_ptr(te, i=i, p=p, j0=j0, j1=j1):
                                W("pe", te, "act",
                                  marks[("a", "exp", i, (j1 - 1) // 8)])
                                WT("pe", te, rel[f"ptp{p % 2}"])
                                for j in range(j0, j1):
                                    inst = te.matmul(
                                        ptp[:, p % 2, j - j0],
                                        p_sb[:, j * 128:(j + 1) * 128],
                                        identb[:], start=(j == j0),
                                        stop=(j == j1 - 1), is_transpose=True)
                                marks[("a", "ptr", i, p)] = M("pe", inst)
                            blk.tensor(t_ptr)

                            def v_ptev(ve, i=i, p=p, j0=j0, j1=j1):
                                W("dve", ve, "pe", marks[("a", "ptr", i, p)])
                                WT("dve", ve, rel[f"ptsb{p % 2}"])
                                m = M("dve", ve.tensor_copy(
                                    pt_sb[:, p % 2, 0:j1 - j0],
                                    ptp[:, p % 2, 0:j1 - j0]))
                                marks[("a", "ptev", i, p)] = m
                                rel[f"ptp{p % 2}"] = ("dve", m)
                            blk.vector(v_ptev)

                            def t_pv(te, i=i, p=p, j0=j0, j1=j1, nkb=nkb):
                                W("pe", te, "dve", marks[("a", "ptev", i, p)])
                                if p == 0:
                                    WT("pe", te, rel["zps_free"])
                                for j in range(j0, j1):
                                    inst = te.matmul(
                                        zps[:], pt_sb[:, p % 2, j - j0],
                                        v_tm[:, j, :], start=(j == 0),
                                        stop=(j == nkb - 1))
                                m = M("pe", inst)
                                marks[("a", "pv", i, p)] = m
                                rel[f"ptsb{p % 2}"] = ("pe", m)
                            blk.tensor(t_pv)

                        def v_z(ve, i=i, npk=npk):
                            W("dve", ve, "pe", marks[("a", "pv", i, npk - 1)])
                            W("dve", ve, "dve", marks[("a", "rs", i)])
                            WT("dve", ve, rel["zsb_free"])
                            m = M("dve", ve.tensor_scalar(
                                z_sb[:], zps[:], srs[:, 0:1], INV_SQRT_HS,
                                ALU.mult, ALU.mult))
                            marks[("a", "z", i)] = m
                            rel["zps_free"] = ("dve", m)
                        blk.vector(v_z)

                        def t_ztr(te, i=i):
                            W("pe", te, "dve", marks[("a", "z", i)])
                            WT("pe", te, rel["ptp0"])
                            for j in range(4):
                                inst = te.matmul(
                                    ptp[:, 0, j], z_sb[:, j * 128:(j + 1) * 128],
                                    identb[:], start=(j == 0), stop=(j == 3),
                                    is_transpose=True)
                            m = M("pe", inst)
                            marks[("a", "ztr", i)] = m
                            rel["zsb_free"] = ("pe", m)
                        blk.tensor(t_ztr)

                        def v_ztev(ve, i=i):
                            W("dve", ve, "pe", marks[("a", "ztr", i)])
                            WT("dve", ve, rel["zt_free"])
                            m = M("dve", ve.tensor_copy(zt_sb[:], ptp[:, 0, 0:4]))
                            marks[("a", "ztev", i)] = m
                            rel["ptp0"] = ("dve", m)
                        blk.vector(v_ztev)

                        def t_op(te, i=i):
                            W("pe", te, "dve", marks[("a", "ztev", i)])
                            if i == 0:
                                W("pe", te, "spj", marks["projsum_done"])
                            WT("pe", te, rel["scp0"])
                            WT("pe", te, rel["scp1"])
                            for vc in range(4):
                                for hb in range(4):
                                    inst = te.matmul(
                                        scp[:, hb // 2,
                                            (hb % 2) * 512:(hb % 2) * 512 + 512],
                                        zt_sb[:, vc, :],
                                        projsum[:, vc, hb * 512:(hb + 1) * 512],
                                        start=(vc == 0), stop=(vc == 3))
                            m = M("pe", inst)
                            marks[("a", "op", i)] = m
                            rel["zt_free"] = ("pe", m)
                        blk.tensor(t_op)


                    emit_scores(0)
                    for i in range(NT_OWN):
                        emit_softmax(i)
                        if i + 1 < NT_OWN:
                            emit_scores(i + 1)
                        emit_pv(i)

                    def a_af_last(sc):
                        W("act", sc, "pe", marks[("a", "op", NT_OWN - 1)])
                        W("act", sc, f"gaf{(NT_OWN - 1) % 2}",
                          marks.get(("a", "afs", NT_OWN - 3)))
                        sc.copy(af_sb[:, (NT_OWN - 1) % 2, 0:1024], scp[:, 0, :])
                        inst = sc.copy(af_sb[:, (NT_OWN - 1) % 2, 1024:2048],
                                       scp[:, 1, :])
                        marks[("a", "af", NT_OWN - 1)] = M("act", inst)
                    blk.scalar(a_af_last)

                    def g_afs_last(sy):
                        W("sq", sy, "act", marks[("a", "af", NT_OWN - 1)])
                        inst = sy.dma_start(
                            out=att16[(NT_OWN - 1) * 128:NT_OWN * 128, :],
                            in_=af_sb[:, (NT_OWN - 1) % 2, :])
                        marks[("a", "afs", NT_OWN - 1)] = cx.dmark(
                            f"gaf{(NT_OWN - 1) % 2}", inst)
                    blk.sync(g_afs_last)

            att_end = cx.snap()

        with ExitStack() as pf:
            w0s = sbt(pf, "w0s", [128, 2, FC, 512], BF16)
            dt = sbt(pf, "dt", [128, 64, 512], BF16)
            xng = sbt(pf, "xngf", [128, 2, FC, 512], BF16)
            w1s = sbt(pf, "w1s", [128, 4, 1024], BF16)
            ef_sb = sbt(pf, "ef_sb", [128, 4, 1024], F16)
            xo_sb = sbt(pf, "xo_sb", [128, 2, 1024], F32)
            at_sb = sbt(pf, "at_sb", [128, 2, 1024], F16)
            of_sb = sbt(pf, "of_sb", [128, 2, 1024], F32)
            b1_r = sbt(pf, "b1_r", [1, D], BF16)
            fps = pf.enter_context(nc.psum_tensor("fps", [128, 4, 1024], F32))

            def dview(r):
                return fps[:, r // 2, (r % 2) * 512:(r % 2) * 512 + 512]

            first = {e: True for e in ("pe", "dve", "act", "gq", "sq")}

            def fbar(name, h):
                if first[name]:
                    cx.barrier(name, h, att_end)
                    first[name] = False

            def seqkey(s):
                if s < 0:
                    return None
                return (s // 8, (s // 4) % 2, s % 4)

            if True:
                def g_b1(gp):
                    fbar("gq", gp)
                    marks["b1_done"] = cx.dmark("sb1", gp.dma_start(
                        out=b1_r[:], in_=b1))
                blk.gpsimd(g_b1)

                for g in range(4):
                    def s_xng(sy, g=g):
                        fbar("sq", sy)
                        W("sq", sy, "gxn0", marks[("q", "xnship", 3)])
                        W("sq", sy, "pe", marks.get(("p1g", g - 2)))
                        inst = sy.dma_start(
                            out=xng[:, g % 2].rearrange("p a b -> p (a b)"),
                            in_=xnt_d[g])
                        marks[("xng", g)] = cx.dmark(f"sxg{g % 2}", inst)
                    blk.sync(s_xng)

                    for nbg in range(16):
                        def s_w0t(sy, g=g, nbg=nbg):
                            if g == 0 and nbg == 0:
                                W("sq", sy, "gw0", marks["gw0_fin"])
                            if nbg >= 2:
                                W("sq", sy, "pe", marks.get(("p1", g, nbg - 2)))
                            elif nbg == 0:
                                W("sq", sy, "pe", marks.get(("p1", g - 1, 14)))
                            else:
                                W("sq", sy, "pe", marks.get(("p1", g - 1, 15)))
                            for fc in range(FC):
                                cx.dmark(f"sw0{nbg % 2}", sy.dma_start(
                                    out=w0s[:, nbg % 2, fc, :],
                                    in_=w0tb[fc * 128:(fc + 1) * 128,
                                             nbg * 512:(nbg + 1) * 512]))
                            marks[("w0t", g, nbg)] = cx.cnt[f"sw0{nbg % 2}"]
                        blk.sync(s_w0t)

                        def t_p1(te, g=g, nbg=nbg):
                            fbar("pe", te)
                            W("pe", te, f"sw0{nbg % 2}", marks[("w0t", g, nbg)])
                            W("pe", te, f"sxg{g % 2}", marks[("xng", g)])
                            for nc8 in range(4):
                                ncg = nbg * 4 + nc8
                                r = ncg % 8
                                if nbg < 2:
                                    W("pe", te, "act",
                                      marks.get(("eev", g - 1, 1, r // 2)))
                                W("pe", te, "act", marks.get(("gelu", g, ncg - 8)))
                                for fc in range(FC):
                                    inst = te.matmul(
                                        dview(r),
                                        w0s[:, nbg % 2, fc, nc8 * 128:(nc8 + 1) * 128],
                                        xng[:, g % 2, fc, :],
                                        start=(fc == 0), stop=(fc == FC - 1))
                                marks[("p1mm", g, ncg)] = M("pe", inst)
                            marks[("p1", g, nbg)] = cx.cnt["pe"]
                        blk.tensor(t_p1)

                        if nbg % 2 == 1:
                            def a_gelu(sc, g=g, nbg=nbg):
                                fbar("act", sc)
                                W("act", sc, "pe", marks.get(("p2last", g - 1)))
                                for ncg in range((nbg - 1) * 4, (nbg + 1) * 4):
                                    W("act", sc, "pe", marks[("p1mm", g, ncg)])
                                    marks[("gelu", g, ncg)] = M("act", sc.activation(
                                        dt[:, ncg, :], dview(ncg % 8), AF.Gelu,
                                        bias=b0t[:, ncg:ncg + 1]))
                            blk.scalar(a_gelu)

                    marks[("p1g", g)] = cx.cnt["pe"]

                    if debug and g == 0:
                        def g_ddbg(gp):
                            W("gq", gp, "act", marks[("gelu", 0, 63)])
                            cx.dmark("gdbg", gp.dma_start(
                                out=ddbg, in_=dt.ap().rearrange("p a b -> p (a b)")))
                        blk.gpsimd(g_ddbg)

                    for hh in range(2):
                        for nci in range(64):
                            def s_w1(sy, g=g, hh=hh, nci=nci):
                                W("sq", sy, "gw1", marks[("w1b", nci // 8)])
                                W("sq", sy, "pe",
                                  marks.get(("p2mm", g, hh, nci - 4)))
                                if nci < 4 and hh == 1:
                                    W("sq", sy, "pe",
                                      marks.get(("p2mm", g, 0, 60 + nci)))
                                elif nci < 4 and hh == 0:
                                    W("sq", sy, "pe",
                                      marks.get(("p2mm", g - 1, 1, 60 + nci)))
                                inst = sy.dma_start(
                                    out=w1s[:, nci % 4, :],
                                    in_=w1tb[nci * 128:(nci + 1) * 128,
                                             hh * 1024:(hh + 1) * 1024])
                                marks[("w1t", g, hh, nci)] = cx.dmark(
                                    f"sw1{nci % 4}", inst)
                            blk.sync(s_w1)

                            def t_p2(te, g=g, hh=hh, nci=nci):
                                W("pe", te, f"sw1{nci % 4}",
                                  marks[("w1t", g, hh, nci)])
                                if nci == 0:
                                    W("pe", te, "sb1", marks["b1_done"])
                                    W("pe", te, "act", marks[("gelu", g, 63)])
                                for tt in range(4):
                                    if nci == 0:
                                        W("pe", te, "act",
                                          marks.get(("eev", g, hh - 1, tt)))
                                        W("pe", te, "act",
                                          marks.get(("eev", g - 1, 1, tt)))
                                    te.matmul(fps[:, tt, 0:512],
                                              dt[:, nci, tt * 128:(tt + 1) * 128],
                                              w1s[:, nci % 4, 0:512],
                                              start=(nci == 0), stop=(nci == 63))
                                    inst = te.matmul(
                                        fps[:, tt, 512:1024],
                                        dt[:, nci, tt * 128:(tt + 1) * 128],
                                        w1s[:, nci % 4, 512:1024],
                                        start=(nci == 0), stop=(nci == 63))
                                    if nci == 0:
                                        te.matmul(
                                            fps[:, tt, 0:512], ones_r[:],
                                            b1_r[0:1, hh * 1024:hh * 1024 + 512],
                                            start=False, stop=False)
                                        inst = te.matmul(
                                            fps[:, tt, 512:1024], ones_r[:],
                                            b1_r[0:1,
                                                 hh * 1024 + 512:hh * 1024 + 1024],
                                            start=False, stop=False)
                                marks[("p2mm", g, hh, nci)] = M("pe", inst)
                            blk.tensor(t_p2)

                        marks[("p2last", g)] = cx.cnt["pe"]

                        def a_eev_all(sc, g=g, hh=hh):
                            W("act", sc, "pe", marks[("p2mm", g, hh, 63)])
                            for tt in range(4):
                                k = seqkey((g * 2 + hh) * 4 + tt - 8)
                                if k is not None:
                                    W("act", sc, "dve", marks.get(("fadd",) + k))
                                marks[("eev", g, hh, tt)] = M("act", sc.copy(
                                    ef_sb[:, tt, :], fps[:, tt, :]))
                        blk.scalar(a_eev_all)

                        for tt in range(4):
                            s = (g * 2 + hh) * 4 + tt

                            def s_ld(sc, g=g, hh=hh, tt=tt, s=s):
                                k = seqkey(s - 2)
                                if k is not None:
                                    W("act", sc, "dve", marks.get(("fadd",) + k))
                                r0 = g * 512 + tt * 128
                                cx.dmark(f"sxo{tt % 2}", sc.dma_start(
                                    out=xo_sb[:, tt % 2, :],
                                    in_=xo[r0:r0 + 128,
                                           hh * 1024:(hh + 1) * 1024]))
                                inst = sc.dma_start(
                                    out=at_sb[:, tt % 2, :],
                                    in_=att16[r0:r0 + 128,
                                              hh * 1024:(hh + 1) * 1024])
                                marks[("xoat", g, hh, tt)] = cx.dmark(
                                    f"sxo{tt % 2}", inst)
                            blk.scalar(s_ld)


                            if debug:
                                def g_edbg(gp, g=g, hh=hh, tt=tt):
                                    W("gq", gp, "act", marks[("eev", g, hh, tt)])
                                    r0 = g * 512 + tt * 128
                                    inst = gp.dma_start(
                                        out=edbg[r0:r0 + 128,
                                                 hh * 1024:(hh + 1) * 1024],
                                        in_=ef_sb[:, tt, :])
                                    marks[("edbg", g, hh, tt)] = cx.dmark(
                                        f"ged{tt % 2}", inst)
                                blk.gpsimd(g_edbg)

                            def v_fadd(ve, g=g, hh=hh, tt=tt, s=s):
                                fbar("dve", ve)
                                W("dve", ve, f"sxo{tt % 2}",
                                  marks[("xoat", g, hh, tt)])
                                W("dve", ve, "act", marks[("eev", g, hh, tt)])
                                k = seqkey(s - 2)
                                if k is not None:
                                    W("dve", ve, f"go{tt % 2}",
                                      marks.get(("ost",) + k))
                                m = M("dve", ve.tensor_add(
                                    of_sb[:, tt % 2, :], xo_sb[:, tt % 2, :],
                                    at_sb[:, tt % 2, :]))
                                W("dve", ve, "dve", m)
                                marks[("fadd", g, hh, tt)] = M("dve", ve.tensor_add(
                                    of_sb[:, tt % 2, :], of_sb[:, tt % 2, :],
                                    ef_sb[:, tt, :]))
                            blk.vector(v_fadd)

                            def g_ost(sc, g=g, hh=hh, tt=tt):
                                W("act", sc, "dve", marks[("fadd", g, hh, tt)])
                                r0 = g * 512 + tt * 128
                                inst = sc.dma_start(
                                    out=out[r0:r0 + 128,
                                            hh * 1024:(hh + 1) * 1024],
                                    in_=of_sb[:, tt % 2, :])
                                marks[("ost", g, hh, tt)] = cx.dmark(
                                    f"go{tt % 2}", inst)
                            blk.scalar(g_ost)

                def g_fin(gp):
                    for e in sorted(cx.sem):
                        if cx.cnt[e] > 0:
                            gp.wait_ge(cx.sem[e], cx.cnt[e])
                blk.gpsimd(g_fin)

    return nc


def make_inputs_for_core(c, inp):
    b, h = c // 2, c % 2
    x = np.asarray(inp["L0_attn_x"])
    xfull = np.ascontiguousarray(x[b])
    xown = np.ascontiguousarray(x[b].reshape(32, 128, D)[h::2].reshape(OWN, D))
    p = np.arange(128)[:, None]
    j = np.arange(256)[None, :]
    cmaskv = np.where(j > 128 * h + p, np.float32(NEG), np.float32(0.0))
    return {
        "x_full": xfull,
        "x_own": xown,
        "cmask": np.ascontiguousarray(cmaskv.astype(np.float32)),
        "wqt": np.ascontiguousarray(np.asarray(inp["L0_attn_WQ"]).T),
        "bq": np.asarray(inp["L0_attn_bQ"]).reshape(1, D),
        "wkt": np.ascontiguousarray(np.asarray(inp["L0_attn_WK"]).T),
        "bkt": np.ascontiguousarray(
            np.asarray(inp["L0_attn_bK"]).reshape(4, 128).T),
        "wvt": np.ascontiguousarray(np.asarray(inp["L0_attn_WV"]).T),
        "bv": np.asarray(inp["L0_attn_bV"]).reshape(1, GD),
        "proj": np.asarray(inp["L0_attn_proj"]),
        "w0t": np.ascontiguousarray(np.asarray(inp["L0_ffn_W0"]).T),
        "b0": np.ascontiguousarray(
            np.asarray(inp["L0_ffn_bias0"]).reshape(64, 128).T),
        "w1t": np.ascontiguousarray(np.asarray(inp["L0_ffn_W1"]).T),
        "b1": np.asarray(inp["L0_ffn_bias1"]).reshape(1, D),
    }


def kernel(**inputs):
    nc = build_program()
    in_maps = [make_inputs_for_core(c, inputs) for c in range(NCORES)]
    res = run_bass_kernel_spmd(nc, in_maps, core_ids=list(range(NCORES)))
    outa = np.empty((B, T, D), dtype=np.float32)
    for c in range(NCORES):
        b, h = c // 2, c % 2
        oc = np.asarray(res.results[c]["out_own"]).reshape(NT_OWN, 128, D)
        outa[b].reshape(32, 128, D)[h::2] = oc
    return outa


# version 23
# speedup vs baseline: 1.1869x; 1.1869x over previous
import numpy as np
import ml_dtypes
from contextlib import ExitStack

import concourse.bass as bass
import concourse.mybir as mybir
from concourse.bass_utils import run_bass_kernel_spmd

F32 = mybir.dt.float32
BF16 = mybir.dt.bfloat16
F16 = mybir.dt.float16
AF = mybir.ActivationFunctionType
ALU = mybir.AluOpType
AX = mybir.AxisListType

B, T, D = 4, 4096, 2048
NCORES = 8
OWN = 2048
NT_OWN = 16
NT_FULL = 32
FC = 16
GD = 512
NFF = 8192
INV_SQRT_HS = 0.08838834764831845
NEG = -1.0e30


class Ctx:

    def __init__(self, nc, stack):
        self.nc = nc
        self.stack = stack
        self.sem = {}
        self.cnt = {}
        self.lastw = {}
        for e in ("pe", "dve", "act", "gpc"):
            self._new(e)

    def _new(self, name):
        self.sem[name] = self.stack.enter_context(nc_semaphore(self.nc, name))
        self.cnt[name] = 0

    def ensure(self, name):
        if name not in self.sem:
            self._new(name)
        return self.sem[name]

    def mark(self, e, inst, k=1):
        inst.then_inc(self.sem[e], k)
        self.cnt[e] += k
        return self.cnt[e]

    def dmark(self, name, inst):
        self.ensure(name)
        return self.mark(name, inst, 16)

    def wait(self, wname, handle, prod, v):
        if v is None or v <= 0 or prod not in self.sem:
            return
        assert v <= self.cnt[prod], f"future wait {prod} {v} > {self.cnt[prod]}"
        if v > self.lastw.get((wname, prod), 0):
            handle.wait_ge(self.sem[prod], v)
            self.lastw[(wname, prod)] = v

    def waitt(self, wname, handle, tup):
        if tup is not None:
            self.wait(wname, handle, tup[0], tup[1])

    def snap(self):
        return dict(self.cnt)

    BARRIER_SKIP = ("gw0", "gw1")

    def barrier(self, wname, handle, snap):
        for e, v in snap.items():
            if e in self.BARRIER_SKIP:
                continue
            self.wait(wname, handle, e, v)


def nc_semaphore(nc, name):
    return nc.semaphore("sem_" + name)


def build_program(debug=False):
    nc = bass.Bass("TRN2", target_bir_lowering=False)

    xf = nc.dram_tensor("x_full", [T, D], F32, kind="ExternalInput").ap()
    xo = nc.dram_tensor("x_own", [OWN, D], F32, kind="ExternalInput").ap()
    cmask = nc.dram_tensor("cmask", [128, 256], F32, kind="ExternalInput").ap()
    wqt = nc.dram_tensor("wqt", [D, D], F32, kind="ExternalInput").ap()
    bqv = nc.dram_tensor("bq", [1, D], F32, kind="ExternalInput").ap()
    wkt = nc.dram_tensor("wkt", [D, GD], F32, kind="ExternalInput").ap()
    bkt = nc.dram_tensor("bkt", [128, 4], F32, kind="ExternalInput").ap()
    wvt = nc.dram_tensor("wvt", [D, GD], F32, kind="ExternalInput").ap()
    bvr = nc.dram_tensor("bv", [1, GD], F32, kind="ExternalInput").ap()
    pj = nc.dram_tensor("proj", [D, D], F32, kind="ExternalInput").ap()
    w0t = nc.dram_tensor("w0t", [D, NFF], F32, kind="ExternalInput").ap()
    b0 = nc.dram_tensor("b0", [128, 64], F32, kind="ExternalInput").ap()
    w1t = nc.dram_tensor("w1t", [NFF, D], F32, kind="ExternalInput").ap()
    b1 = nc.dram_tensor("b1", [1, D], F32, kind="ExternalInput").ap()
    out = nc.dram_tensor("out_own", [OWN, D], F32, kind="ExternalOutput").ap()

    ikind = "ExternalOutput" if debug else "Internal"
    projb = nc.dram_tensor("projb", [GD, D], BF16, kind=ikind).ap()
    bqd = nc.dram_tensor("bqd", [1, GD], F32, kind="Internal").ap()
    w0tb = nc.dram_tensor("w0tb", [D, NFF], BF16, kind="Internal").ap()
    w1tb = nc.dram_tensor("w1tb", [NFF, D], BF16, kind="Internal").ap()
    xnt_d = nc.dram_tensor("xnt_d", [4, 128, FC * 512], BF16, kind=ikind).ap()
    att16 = nc.dram_tensor("att16", [OWN, D], F16, kind=ikind).ap()
    if debug:
        kdbg = nc.dram_tensor("kdbg", [128, 4 * T], BF16, kind="ExternalOutput").ap()
        vdbg = nc.dram_tensor("vdbg", [128, 32 * 512], BF16, kind="ExternalOutput").ap()
        qdbg = nc.dram_tensor("qdbg", [128, 4 * OWN], BF16, kind="ExternalOutput").ap()
        ddbg = nc.dram_tensor("ddbg", [128, 64 * 512], BF16, kind="ExternalOutput").ap()
        edbg = nc.dram_tensor("edbg", [OWN, D], F16, kind="ExternalOutput").ap()

    with ExitStack() as outer:
        cx = Ctx(nc, outer)
        blk = outer.enter_context(nc.Block())
        W = cx.wait
        WT = cx.waitt
        M = cx.mark
        marks = {}

        def sbt(st, n, s, d):
            return st.enter_context(nc.sbuf_tensor(n, s, d))

        identb = sbt(outer, "identb", [128, 128], BF16)
        ident32 = sbt(outer, "ident32", [4, 4], F32)
        ones_r = sbt(outer, "ones_r", [1, 128], BF16)
        bv_r = sbt(outer, "bv_r", [1, GD], BF16)
        bkt_sb = sbt(outer, "bkt_sb", [128, 4], F32)
        bqt_sb = sbt(outer, "bqt_sb", [128, 4], F32)
        b0t = sbt(outer, "b0t", [128, 64], F32)
        cm_sb = sbt(outer, "cm_sb", [128, 256], F32)
        stats = sbt(outer, "stats", [128, 2, 24], F32)
        smax = sbt(outer, "smax", [128, 1], F32)
        pmax = sbt(outer, "pmax", [128, 4], F32)
        ssum = sbt(outer, "ssum", [128, 1], F32)
        srs = sbt(outer, "srs", [128, 1], F32)
        esum = sbt(outer, "esum", [128, 4], F32)
        accw = sbt(outer, "accw", [128, D], F32)
        bq4_sb = sbt(outer, "bq4_sb", [4, 128], F32)

        if True:
            def g_p0a(gp):
                gp.memset(ones_r[:], 1.0)
                m = M("gpc", gp.memset(identb[:], 0.0))
                gp.wait_ge(cx.sem["gpc"], m)
                inst = gp.affine_select(
                    out=identb[:], in_=identb[:], compare_op=ALU.not_equal,
                    fill=1.0, base=0, pattern=[[-1, 128]], channel_multiplier=1)
                m = M("gpc", gp.memset(ident32[:], 0.0))
                gp.wait_ge(cx.sem["gpc"], m)
                inst = gp.affine_select(
                    out=ident32[:], in_=ident32[:], compare_op=ALU.not_equal,
                    fill=1.0, base=0, pattern=[[-1, 4]], channel_multiplier=1)
                marks["p0_consts"] = M("gpc", inst)
                for ap_out, ap_in in ((bv_r[:], bvr), (bkt_sb[:], bkt),
                                      (b0t[:], b0), (cm_sb[:], cmask)):
                    cx.dmark("gin", gp.dma_start(out=ap_out, in_=ap_in))
                marks["p0_loads"] = cx.cnt["gin"]
            blk.gpsimd(g_p0a)

        def g_p0b(gp):
            for r in range(4):
                op = ALU.bypass if r == 0 else ALU.add
                cx.dmark("gbq", gp.dma_start(
                    out=accw[0:1, 0:GD], in_=bqv[0:1, r * GD:(r + 1) * GD], accum_op=op))
                gp.wait_ge(cx.sem["gbq"], cx.cnt["gbq"])
            cx.dmark("gbq", gp.dma_start(out=bqd, in_=accw[0:1, 0:GD]))
            marks["gbq_fin"] = cx.cnt["gbq"]
            cpm = {}
            for base in range(0, FC, 4):
                for r in range(4):
                    for j in range(4):
                        fc = base + j
                        sl = f"gac{j}"
                        cx.ensure(sl)
                        if cx.cnt[sl] > 0:
                            gp.wait_ge(cx.sem[sl], cx.cnt[sl])
                        if r == 0 and j in cpm:
                            gp.wait_ge(cx.sem["gpc"], cpm[j])
                        op = ALU.bypass if r == 0 else ALU.add
                        cx.dmark(sl, gp.dma_start(
                            out=accw[:, j * GD:(j + 1) * GD],
                            in_=wqt[fc * 128:(fc + 1) * 128,
                                    r * GD:(r + 1) * GD],
                            accum_op=op))
                for j in range(4):
                    fc = base + j
                    sl = f"gac{j}"
                    gp.wait_ge(cx.sem[sl], cx.cnt[sl])
                    cpm[j] = M("gpc", gp.tensor_copy(
                        wqtb_sb[:, fc, :], accw[:, j * GD:(j + 1) * GD]))
            marks["wqt_done_gpc"] = cx.cnt["gpc"]
            for j in range(4):
                marks[f"gac{j}_fin"] = cx.cnt[f"gac{j}"]

        def g_pjfold(gp):
            gp.wait_ge(cx.sem["gpc"], marks["wqt_done_gpc"])
            for j in range(4):
                gp.wait_ge(cx.sem[f"gac{j}"], cx.cnt[f"gac{j}"])
            cx.ensure("gpj0")
            cx.ensure("gpj1")
            for g in range(4):
                for r in range(4):
                    op = ALU.bypass if r == 0 else ALU.add
                    for h in range(2):
                        sl = f"gpj{h}"
                        if cx.cnt[sl] > 0:
                            gp.wait_ge(cx.sem[sl], cx.cnt[sl])
                        cx.dmark(sl, gp.dma_start(
                            out=accw[:, h * 1024:(h + 1) * 1024],
                            in_=pj[(g + 4 * r) * 128:(g + 4 * r + 1) * 128,
                                   h * 1024:(h + 1) * 1024],
                            accum_op=op))
                for h in range(2):
                    sl = f"gpj{h}"
                    gp.wait_ge(cx.sem[sl], cx.cnt[sl])
                    cx.dmark(sl, gp.dma_start(
                        out=projb[g * 128:(g + 1) * 128,
                                  h * 1024:(h + 1) * 1024],
                        in_=accw[:, h * 1024:(h + 1) * 1024]))
            marks["gpj_fin0"] = cx.cnt["gpj0"]
            marks["gpj_fin1"] = cx.cnt["gpj1"]

        with ExitStack() as span:
            k_fm = sbt(span, "k_fm", [128, 4, T], BF16)
            v_tm = sbt(span, "v_tm", [128, 32, 512], BF16)
            q_fm = sbt(span, "q_fm", [128, 4, OWN], BF16)
            wqtb_sb = sbt(span, "wqt_sb", [128, FC, 512], BF16)

            with ExitStack() as pb:
                x_sb = sbt(pb, "x_sb", [128, 4, D], F32)
                xn_sem = {"f": "dve", "q": "dve"}
                xn_sb = sbt(pb, "xn_sb", [128, 4, D], BF16)
                xng_sb = sbt(pb, "xng", [128, FC, 512], BF16)
                sq_scr = sbt(pb, "sq_scr", [128, D], BF16)

                def emit_loads(g4, src, tag, trange, ebar=None):
                    base = 4 * g4

                    def emit_on(tlist, bfn, wname):
                        if not tlist:
                            return

                        def _ld(h):
                            if base == 0 and ebar is not None:
                                cx.barrier(wname, h, ebar)
                            for t in tlist:
                                i = base + t
                                sl = i % 4
                                W(wname, h, "dve", marks.get((tag, "red", i - 5)))
                                W(wname, h, "act", marks.get((tag, "sq", i - 5)))
                                W(wname, h, xn_sem[tag],
                                  marks.get((tag, "xnrun", (i - 5) // 4))
                                  if i >= 5 else None)
                                inst = h.dma_start(
                                    out=x_sb[:, sl, :],
                                    in_=src[i * 128:(i + 1) * 128, :])
                                marks[(tag, "xload", i)] = cx.dmark(f"sx{sl}", inst)
                        bfn(_ld)
                    emit_on([t for t in trange if t % 2 == 0], blk.sync, "sq")
                    emit_on([t for t in trange if t % 2 == 1], blk.scalar, "act")

                def emit_sqrun(g4, tag, ebar=None):
                    gb = g4 % 2
                    stg = stats[:, gb, :]
                    base = 4 * g4

                    def a_sq(sc):
                        if base == 0 and ebar is not None:
                            cx.barrier("act", sc, ebar)
                        if base == 0 and tag == "f":
                            W("act", sc, "gpc", marks["p0_consts"])
                        W("act", sc, "dve", marks.get((tag, "stats", g4 - 2)))
                        for t in range(4):
                            i = base + t
                            W("act", sc, f"sx{i % 4}", marks[(tag, "xload", i)])
                            if t > 0:
                                W("act", sc, "act", marks[(tag, "sq", i - 1)])
                            marks[(tag, "sq", i)] = M("act", sc.activation(
                                sq_scr[:], x_sb[:, i % 4, :], AF.Square,
                                accum_out=stg[:, 4 + t:5 + t]))
                        marks[(tag, "sqrun", g4)] = marks[(tag, "sq", base + 3)]
                    blk.scalar(a_sq)

                def emit_vpre(g4, tag, ebar=None):
                    gb = g4 % 2
                    stg = stats[:, gb, :]
                    base = 4 * g4

                    def v_pre(ve):
                        if base == 0 and ebar is not None:
                            cx.barrier("dve", ve, ebar)
                        W("dve", ve, xn_sem[tag],
                          marks.get((tag, "xnrun", g4 - 2)))
                        for t in range(4):
                            i = base + t
                            W("dve", ve, f"sx{i % 4}", marks[(tag, "xload", i)])
                            marks[(tag, "red", i)] = M("dve", ve.reduce_sum(
                                stg[:, t:t + 1], x_sb[:, i % 4, :], axis=AX.X))
                        W("dve", ve, "dve", marks[(tag, "red", base + 3)])
                        W("dve", ve, "act", marks[(tag, "sqrun", g4)])
                        M("dve", ve.tensor_scalar_mul(
                            stg[:, 8:12], stg[:, 0:4], -1.0 / D))
                        m = M("dve", ve.tensor_scalar(
                            stg[:, 12:16], stg[:, 4:8], 1.0 / D, 1.0e-5,
                            ALU.mult, ALU.add))
                        W("dve", ve, "dve", m)
                        m = M("dve", ve.tensor_mul(
                            stg[:, 16:20], stg[:, 8:12], stg[:, 8:12]))
                        W("dve", ve, "dve", m)
                        m = M("dve", ve.tensor_sub(
                            stg[:, 12:16], stg[:, 12:16], stg[:, 16:20]))
                        W("dve", ve, "dve", m)
                        marks[(tag, "stats", g4)] = M("dve", ve.reciprocal(
                            stg[:, 16:20], stg[:, 12:16]))
                    blk.vector(v_pre)

                def emit_arun(g4, src, tag, ngroups, ebar=None):
                    gb = g4 % 2
                    stg = stats[:, gb, :]
                    base = 4 * g4

                    def a_run(sc):
                        W("act", sc, "dve", marks[(tag, "stats", g4)])
                        W("act", sc, "dve", marks.get((tag, "xnrun", g4 - 2)))
                        marks[(tag, "lnrun", g4)] = M("act", sc.activation(
                            stg[:, 20:24], stg[:, 16:20], AF.Sqrt))
                    blk.scalar(a_run)


                def emit_tiles(g4, tag, trp, vps, with_v, nxt=None):
                    gb = g4 % 2
                    stg = stats[:, gb, :]
                    base = 4 * g4
                    for t in range(4):
                        i = base + t

                        def v_xn(ve, t=t, i=i):
                            W("dve", ve, "act", marks[(tag, "lnrun", g4)])
                            W("dve", ve, f"sx{i % 4}", marks[(tag, "xload", i)])
                            W("dve", ve, "pe", marks.get((tag, "tr", i - 4)))
                            inst = ve.tensor_scalar(
                                xn_sb[:, t, :], x_sb[:, i % 4, :],
                                stg[:, 8 + t:9 + t], stg[:, 20 + t:21 + t],
                                ALU.add, ALU.mult)
                            marks[(tag, "xn", i)] = M("dve", inst)
                            if t == 3:
                                marks[(tag, "xnrun", g4)] = marks[(tag, "xn", i)]
                        blk.vector(v_xn)

                        if nxt is not None:
                            i2 = i + 4
                            bfn = blk.scalar if t % 2 == 1 else blk.sync
                            wname = "act" if t % 2 == 1 else "sq"

                            def _ld(h, t=t, i2=i2, wname=wname):
                                W(wname, h, "dve", marks[(tag, "xn", i2 - 4)])
                                W(wname, h, "act", marks[(tag, "sq", i2 - 4)])
                                inst = h.dma_start(
                                    out=x_sb[:, t, :],
                                    in_=nxt[i2 * 128:(i2 + 1) * 128, :])
                                marks[(tag, "xload", i2)] = cx.dmark(
                                    f"sx{t}", inst)
                            bfn(_ld)

                        def t_tr(te, t=t, i=i):
                            if i == 0:
                                W("pe", te, "gpc", marks["p0_consts"])
                            W("pe", te, xn_sem[tag], marks[(tag, "xn", i)])
                            W("pe", te, "dve", marks.get((tag, "trev", i - 1)))
                            for half in range(2):
                                for j in range(8):
                                    c = half * 8 + j
                                    inst = te.matmul(
                                        trp[:, half, j],
                                        xn_sb[:, t, c * 128:(c + 1) * 128],
                                        identb[:], start=(j == 0), stop=(j == 7),
                                        is_transpose=True)
                                marks[(tag, "trh", i, half)] = M("pe", inst)
                            marks[(tag, "tr", i)] = marks[(tag, "trh", i, 1)]
                        blk.tensor(t_tr)

                        def v_trev(ve, t=t, i=i):
                            W("dve", ve, "pe", marks[(tag, "trh", i, 0)])
                            if t == 0:
                                W("dve", ve, "pe", marks.get((tag, "kmm", g4 - 1)))
                                if tag == "q":
                                    W("dve", ve, "gxn0",
                                      marks.get(("q", "xnship", g4 - 1)))
                            src2 = trp.ap().rearrange("p a b c -> p (a b) c")
                            dst = xng_sb[:, :, t * 128:(t + 1) * 128]
                            M("dve", ve.tensor_copy(dst[:, 0:8, :], src2[:, 0:8, :]))
                            W("dve", ve, "pe", marks[(tag, "trh", i, 1)])
                            marks[(tag, "trev", i)] = M("dve", ve.tensor_copy(
                                dst[:, 8:16, :], src2[:, 8:16, :]))
                        blk.vector(v_trev)

                        if with_v:
                            def t_vmm(te, t=t, i=i):
                                W("pe", te, "dve", marks[(tag, "trev", i)])
                                if i == 0:
                                    W("pe", te, "swk", marks["wkvt_v_done"])
                                    W("pe", te, "gin", marks["p0_loads"])
                                W("pe", te, "act", marks.get((tag, "vev", i - 2)))
                                ps = vps[:, i % 2, :]
                                for fc in range(FC):
                                    te.matmul(
                                        ps, xng_sb[:, fc, t * 128:(t + 1) * 128],
                                        wkvt[:, fc, 512:1024],
                                        start=(fc == 0), stop=False)
                                inst = te.matmul(ps, ones_r[:], bv_r[:],
                                                 start=False, stop=True)
                                marks[(tag, "vmm", i)] = M("pe", inst)
                            blk.tensor(t_vmm)

                            def a_vev(sc, t=t, i=i):
                                W("act", sc, "pe", marks[(tag, "vmm", i)])
                                marks[(tag, "vev", i)] = M("act", sc.copy(
                                    v_tm[:, i, :], vps[:, i % 2, :]))
                            blk.scalar(a_vev)

                def emit_kqmm(g4, tag, kps, wt_sb, dst_fm, bias_sb):
                    def t_kmm(te):
                        W("pe", te, "dve", marks[(tag, "trev", 4 * g4 + 3)])
                        W("pe", te, "dve", marks.get((tag, "kev", g4 - 1)))
                        if g4 == 0 and tag == "f":
                            W("pe", te, "swk", marks["wkvt_done"])
                        if g4 == 0 and tag == "q":
                            W("pe", te, "gpc", marks["wqt_done_gpc"])
                        for kc in range(4):
                            for fc in range(FC):
                                inst = te.matmul(
                                    kps[:, kc, :],
                                    wt_sb[:, fc, kc * 128:(kc + 1) * 128],
                                    xng_sb[:, fc, :],
                                    start=(fc == 0), stop=(fc == FC - 1))
                            marks[(tag, "kmmc", g4, kc)] = M("pe", inst)
                        marks[(tag, "kmm", g4)] = marks[(tag, "kmmc", g4, 3)]
                    blk.tensor(t_kmm)

                    def v_kev(ve):
                        if g4 == 0 and tag == "f":
                            W("dve", ve, "gin", marks["p0_loads"])
                        if g4 == 0 and tag == "q":
                            W("dve", ve, "dve", marks["bqt_done"])
                        for kc in range(4):
                            W("dve", ve, "pe", marks[(tag, "kmmc", g4, kc)])
                            inst = ve.tensor_scalar(
                                dst_fm[:, kc, g4 * 512:(g4 + 1) * 512],
                                kps[:, kc, :], bias_sb[:, kc:kc + 1], None,
                                ALU.add)
                            M("dve", inst)
                        marks[(tag, "kev", g4)] = cx.cnt["dve"]
                    blk.vector(v_kev)

                with ExitStack() as pkv:
                    wkvt = sbt(pkv, "wkvt", [128, FC, 1024], BF16)
                    kps = pkv.enter_context(nc.psum_tensor("kps", [128, 4, 512], F32))
                    vps = pkv.enter_context(nc.psum_tensor("vps", [128, 2, 512], F32))
                    trp = pkv.enter_context(nc.psum_tensor("trp", [128, 2, 8, 128], BF16))

                    emit_loads(0, xf, "f", range(4))

                    def g_wkv(gp):
                        for fc in range(FC):
                            cx.dmark("swk", gp.dma_start(
                                out=wkvt[:, fc, 512:1024],
                                in_=wvt[fc * 128:(fc + 1) * 128, :]))
                        marks["wkvt_v_done"] = cx.cnt["swk"]
                        for fc in range(FC):
                            cx.dmark("swk", gp.dma_start(
                                out=wkvt[:, fc, 0:512],
                                in_=wkt[fc * 128:(fc + 1) * 128, :]))
                        marks["wkvt_done"] = cx.cnt["swk"]
                    blk.gpsimd(g_wkv)
                    blk.gpsimd(g_p0b)

                    emit_sqrun(0, "f")
                    for g4 in range(8):
                        emit_vpre(g4, "f")
                        emit_arun(g4, xf, "f", 8)
                        emit_tiles(g4, "f", trp, vps, with_v=True,
                                   nxt=xf if g4 + 1 < 8 else None)
                        if g4 + 1 < 8:
                            emit_sqrun(g4 + 1, "f")
                        emit_kqmm(g4, "f", kps, wkvt, k_fm, bkt_sb)

                blk.gpsimd(g_pjfold)

                kv_end = cx.snap()

                with ExitStack() as pq:
                    qps = pq.enter_context(nc.psum_tensor("qps", [128, 4, 512], F32))
                    trpq = pq.enter_context(nc.psum_tensor("trpq", [128, 2, 8, 128], BF16))
                    bqps = pq.enter_context(nc.psum_tensor("bqps", [128, 4], F32))

                    def s_wq(sy):
                        W("sq", sy, "gbq", marks["gbq_fin"])
                        inst = sy.dma_start(
                            out=bq4_sb[:],
                            in_=bqd.rearrange("1 (a b) -> a b", a=4))
                        marks["bq4"] = cx.dmark("sbq", inst)
                    blk.sync(s_wq)

                    def t_bq(te):
                        cx.barrier("pe", te, kv_end)
                        W("pe", te, "sbq", marks["bq4"])
                        marks["bqtr"] = M("pe", te.transpose(
                            bqps[:], bq4_sb[:], ident32[0:4, 0:4]))
                    blk.tensor(t_bq)

                    def v_bq(ve):
                        cx.barrier("dve", ve, kv_end)
                        W("dve", ve, "pe", marks["bqtr"])
                        marks["bqt_done"] = M("dve", ve.tensor_copy(
                            bqt_sb[:], bqps[:]))
                    blk.vector(v_bq)

                    emit_loads(0, xo, "q", range(4), ebar=kv_end)
                    emit_sqrun(0, "q", ebar=kv_end)
                    for g4 in range(4):
                        emit_vpre(g4, "q", ebar=kv_end)
                        emit_arun(g4, xo, "q", 4, ebar=kv_end)
                        emit_tiles(g4, "q", trpq, None, with_v=False,
                                   nxt=xo if g4 + 1 < 4 else None)
                        if g4 + 1 < 4:
                            emit_sqrun(g4 + 1, "q")
                        emit_kqmm(g4, "q", qps, wqtb_sb, q_fm, bqt_sb)

                        def s_xnship(sy, g4=g4):
                            W("sq", sy, "pe", marks[("q", "kmm", g4)])
                            inst = sy.dma_start(
                                out=xnt_d[g4],
                                in_=xng_sb.ap().rearrange("p a b -> p (a b)"))
                            marks[("q", "xnship", g4)] = cx.dmark("gxn0", inst)
                        blk.sync(s_xnship)

                def g_w0cast(gp):
                    for c in range(4):
                        cx.dmark("gw0", gp.dma_start(
                            out=w0tb[c * 512:(c + 1) * 512, :],
                            in_=w0t[c * 512:(c + 1) * 512, :]))
                    marks["gw0_fin"] = cx.cnt["gw0"]
                blk.gpsimd(g_w0cast)

                def g_w1cast(gp):
                    for c in range(8):
                        cx.dmark("gw1", gp.dma_start(
                            out=w1tb[c * 1024:(c + 1) * 1024, :],
                            in_=w1t[c * 1024:(c + 1) * 1024, :]))
                        marks[("w1b", c)] = cx.cnt["gw1"]
                blk.gpsimd(g_w1cast)

                q_done = cx.snap()

            if debug:
                if True:
                    def g_dbg(gp):
                        W("gq", gp, "act", cx.cnt["act"])
                        W("gq", gp, "dve", cx.cnt["dve"])
                        cx.dmark("gdbg", gp.dma_start(
                            out=kdbg, in_=k_fm.ap().rearrange("p a b -> p (a b)")))
                        cx.dmark("gdbg", gp.dma_start(
                            out=vdbg, in_=v_tm.ap().rearrange("p a b -> p (a b)")))
                        cx.dmark("gdbg", gp.dma_start(
                            out=qdbg, in_=q_fm.ap().rearrange("p a b -> p (a b)")))
                    blk.gpsimd(g_dbg)

            pass_end = cx.snap()

            with ExitStack() as pa:
                sc_sb = sbt(pa, "sc_sb", [128, T], F32)
                p_sb = sbt(pa, "p_sb", [128, T], BF16)
                pt_sb = sbt(pa, "pt_sb", [128, 2, 8, 128], BF16)
                z_sb = sbt(pa, "z_sb", [128, 512], BF16)
                zt_sb = sbt(pa, "zt_sb", [128, 4, 128], BF16)
                af_sb = sbt(pa, "af_sb", [128, 2, D], F16)
                projsum = sbt(pa, "projsum", [128, 4, D], BF16)
                scp = pa.enter_context(nc.psum_tensor("scp", [128, 2, 1024], F32))
                zps = pa.enter_context(nc.psum_tensor("zps", [128, 512], F32))
                ptp = pa.enter_context(nc.psum_tensor("ptp", [128, 2, 8, 128], BF16))

                rel = {"scp0": None, "scp1": None, "zps_free": None,
                       "zsb_free": None, "ptp0": None, "ptp1": None,
                       "ptsb0": None, "ptsb1": None, "zt_free": None}

                def nchunks(i):
                    return (256 * (i + 1) + 1023) // 1024

                if True:
                    def s_pj(sy):
                        cx.barrier("sq", sy, pass_end)
                        W("sq", sy, "gpj0", marks["gpj_fin0"])
                        W("sq", sy, "gpj1", marks["gpj_fin1"])
                        for g in range(4):
                            inst = sy.dma_start(out=projsum[:, g, :],
                                                in_=projb[g * 128:(g + 1) * 128, :])
                            cx.dmark("spj", inst)
                        marks["projsum_done"] = cx.cnt["spj"]
                    blk.sync(s_pj)

                    def emit_scores(i):
                        Ki = 256 * (i + 1)
                        nch = nchunks(i)
                        for c in range(nch):
                            wdt = min(1024, Ki - 1024 * c)

                            def t_sc(te, i=i, c=c, wdt=wdt):
                                if i == 0 and c == 0:
                                    cx.barrier("pe", te, pass_end)
                                WT("pe", te, rel[f"scp{c % 2}"])
                                ps = scp[:, c % 2, :]
                                for qc in range(4):
                                    inst = te.matmul(
                                        ps[:, 0:min(512, wdt)],
                                        q_fm[:, qc, i * 128:(i + 1) * 128],
                                        k_fm[:, qc, 1024 * c:1024 * c + min(512, wdt)],
                                        start=(qc == 0), stop=(qc == 3))
                                    if wdt > 512:
                                        inst = te.matmul(
                                            ps[:, 512:wdt],
                                            q_fm[:, qc, i * 128:(i + 1) * 128],
                                            k_fm[:, qc,
                                                 1024 * c + 512:1024 * c + wdt],
                                            start=(qc == 0), stop=(qc == 3))
                                marks[("a", "sc", i, c)] = M("pe", inst)
                            blk.tensor(t_sc)

                            def a_scev(sc, i=i, c=c, wdt=wdt):
                                if i == 0 and c == 0:
                                    cx.barrier("act", sc, pass_end)
                                W("act", sc, "pe", marks[("a", "sc", i, c)])
                                m = M("act", sc.copy(
                                    sc_sb[:, 1024 * c:1024 * c + wdt],
                                    scp[:, c % 2, 0:wdt]))
                                marks[("a", "scev", i, c)] = m
                                rel[f"scp{c % 2}"] = ("act", m)
                            blk.scalar(a_scev)

                            def v_scev(ve, i=i, c=c, wdt=wdt, nch=nch, Ki=Ki):
                                if i == 0 and c == 0:
                                    cx.barrier("dve", ve, pass_end)
                                W("dve", ve, "act", marks[("a", "scev", i, c)])
                                if c == nch - 1:
                                    m2 = M("dve", ve.tensor_add(
                                        sc_sb[:, Ki - 256:Ki],
                                        sc_sb[:, Ki - 256:Ki], cm_sb[:]))
                                    W("dve", ve, "dve", m2)
                                marks[("a", "pm", i, c)] = M("dve", ve.reduce_max(
                                    pmax[:, c:c + 1],
                                    sc_sb[:, 1024 * c:1024 * c + wdt], axis=AX.X))
                            blk.vector(v_scev)

                    def emit_softmax(i):
                        nch = nchunks(i)
                        Ki = 256 * (i + 1)

                        def v_max(ve, i=i, nch=nch):
                            W("dve", ve, "dve", marks[("a", "pm", i, nch - 1)])
                            W("dve", ve, "act", marks.get(("a", "expl", i - 1)))
                            marks[("a", "max", i)] = M("dve", ve.reduce_max(
                                smax[:, 0:1], pmax[:, 0:nch], axis=AX.X, negate=True))
                        blk.vector(v_max)

                        def a_exp(sc, i=i, Ki=Ki):
                            if i >= 1:
                                W("act", sc, "pe", marks[("a", "op", i - 1)])
                                W("act", sc, f"gaf{(i - 1) % 2}",
                                  marks.get(("a", "afs", i - 3)))
                                sc.copy(af_sb[:, (i - 1) % 2, 0:1024], scp[:, 0, :])
                                inst = sc.copy(af_sb[:, (i - 1) % 2, 1024:2048],
                                               scp[:, 1, :])
                                m = M("act", inst)
                                marks[("a", "af", i - 1)] = m
                                rel["scp0"] = ("act", m)
                                rel["scp1"] = ("act", m)
                            W("act", sc, "dve", marks[("a", "max", i)])
                            W("act", sc, "dve", marks.get(("a", "rs", i - 1)))
                            nch2 = nchunks(i)
                            for c in range(nch2):
                                wdt = min(1024, Ki - 1024 * c)
                                marks[("a", "exp", i, c)] = M("act", sc.activation(
                                    p_sb[:, 1024 * c:1024 * c + wdt],
                                    sc_sb[:, 1024 * c:1024 * c + wdt], AF.Exp,
                                    bias=smax[:, 0:1],
                                    accum_out=esum[:, c:c + 1]))
                            marks[("a", "expl", i)] = marks[("a", "exp", i, nch2 - 1)]
                        blk.scalar(a_exp)

                        def g_afs_prev(sy, i=i):
                            if i >= 1:
                                W("sq", sy, "act", marks[("a", "af", i - 1)])
                                inst = sy.dma_start(
                                    out=att16[(i - 1) * 128:i * 128, :],
                                    in_=af_sb[:, (i - 1) % 2, :])
                                marks[("a", "afs", i - 1)] = cx.dmark(
                                    f"gaf{(i - 1) % 2}", inst)
                        blk.sync(g_afs_prev)

                        def v_rs(ve, i=i, nch=nch):
                            W("dve", ve, "act", marks[("a", "expl", i)])
                            m = M("dve", ve.reduce_sum(
                                ssum[:, 0:1], esum[:, 0:nch], axis=AX.X))
                            W("dve", ve, "dve", m)
                            marks[("a", "rs", i)] = M("dve", ve.reciprocal(
                                srs[:, 0:1], ssum[:, 0:1]))
                        blk.vector(v_rs)

                    def emit_pv(i):
                        nkb = 2 * i + 2
                        npk = (nkb + 7) // 8
                        for p in range(npk):
                            j0, j1 = p * 8, min(nkb, p * 8 + 8)

                            def t_ptr(te, i=i, p=p, j0=j0, j1=j1):
                                W("pe", te, "act",
                                  marks[("a", "exp", i, (j1 - 1) // 8)])
                                WT("pe", te, rel[f"ptp{p % 2}"])
                                for j in range(j0, j1):
                                    inst = te.matmul(
                                        ptp[:, p % 2, j - j0],
                                        p_sb[:, j * 128:(j + 1) * 128],
                                        identb[:], start=(j == j0),
                                        stop=(j == j1 - 1), is_transpose=True)
                                marks[("a", "ptr", i, p)] = M("pe", inst)
                            blk.tensor(t_ptr)

                            def v_ptev(ve, i=i, p=p, j0=j0, j1=j1):
                                W("dve", ve, "pe", marks[("a", "ptr", i, p)])
                                WT("dve", ve, rel[f"ptsb{p % 2}"])
                                m = M("dve", ve.tensor_copy(
                                    pt_sb[:, p % 2, 0:j1 - j0],
                                    ptp[:, p % 2, 0:j1 - j0]))
                                marks[("a", "ptev", i, p)] = m
                                rel[f"ptp{p % 2}"] = ("dve", m)
                            blk.vector(v_ptev)

                            def t_pv(te, i=i, p=p, j0=j0, j1=j1, nkb=nkb):
                                W("pe", te, "dve", marks[("a", "ptev", i, p)])
                                if p == 0:
                                    WT("pe", te, rel["zps_free"])
                                for j in range(j0, j1):
                                    inst = te.matmul(
                                        zps[:], pt_sb[:, p % 2, j - j0],
                                        v_tm[:, j, :], start=(j == 0),
                                        stop=(j == nkb - 1))
                                m = M("pe", inst)
                                marks[("a", "pv", i, p)] = m
                                rel[f"ptsb{p % 2}"] = ("pe", m)
                            blk.tensor(t_pv)

                        def v_z(ve, i=i, npk=npk):
                            W("dve", ve, "pe", marks[("a", "pv", i, npk - 1)])
                            W("dve", ve, "dve", marks[("a", "rs", i)])
                            WT("dve", ve, rel["zsb_free"])
                            m = M("dve", ve.tensor_scalar(
                                z_sb[:], zps[:], srs[:, 0:1], INV_SQRT_HS,
                                ALU.mult, ALU.mult))
                            marks[("a", "z", i)] = m
                            rel["zps_free"] = ("dve", m)
                        blk.vector(v_z)

                        def t_ztr(te, i=i):
                            W("pe", te, "dve", marks[("a", "z", i)])
                            WT("pe", te, rel["ptp0"])
                            for j in range(4):
                                inst = te.matmul(
                                    ptp[:, 0, j], z_sb[:, j * 128:(j + 1) * 128],
                                    identb[:], start=(j == 0), stop=(j == 3),
                                    is_transpose=True)
                            m = M("pe", inst)
                            marks[("a", "ztr", i)] = m
                            rel["zsb_free"] = ("pe", m)
                        blk.tensor(t_ztr)

                        def v_ztev(ve, i=i):
                            W("dve", ve, "pe", marks[("a", "ztr", i)])
                            WT("dve", ve, rel["zt_free"])
                            m = M("dve", ve.tensor_copy(zt_sb[:], ptp[:, 0, 0:4]))
                            marks[("a", "ztev", i)] = m
                            rel["ptp0"] = ("dve", m)
                        blk.vector(v_ztev)

                        def t_op(te, i=i):
                            W("pe", te, "dve", marks[("a", "ztev", i)])
                            if i == 0:
                                W("pe", te, "spj", marks["projsum_done"])
                            WT("pe", te, rel["scp0"])
                            WT("pe", te, rel["scp1"])
                            for vc in range(4):
                                for hb in range(4):
                                    inst = te.matmul(
                                        scp[:, hb // 2,
                                            (hb % 2) * 512:(hb % 2) * 512 + 512],
                                        zt_sb[:, vc, :],
                                        projsum[:, vc, hb * 512:(hb + 1) * 512],
                                        start=(vc == 0), stop=(vc == 3))
                            m = M("pe", inst)
                            marks[("a", "op", i)] = m
                            rel["zt_free"] = ("pe", m)
                        blk.tensor(t_op)


                    emit_scores(0)
                    for i in range(NT_OWN):
                        emit_softmax(i)
                        if i + 1 < NT_OWN:
                            emit_scores(i + 1)
                        emit_pv(i)

                    def a_af_last(sc):
                        W("act", sc, "pe", marks[("a", "op", NT_OWN - 1)])
                        W("act", sc, f"gaf{(NT_OWN - 1) % 2}",
                          marks.get(("a", "afs", NT_OWN - 3)))
                        sc.copy(af_sb[:, (NT_OWN - 1) % 2, 0:1024], scp[:, 0, :])
                        inst = sc.copy(af_sb[:, (NT_OWN - 1) % 2, 1024:2048],
                                       scp[:, 1, :])
                        marks[("a", "af", NT_OWN - 1)] = M("act", inst)
                    blk.scalar(a_af_last)

                    def g_afs_last(sy):
                        W("sq", sy, "act", marks[("a", "af", NT_OWN - 1)])
                        inst = sy.dma_start(
                            out=att16[(NT_OWN - 1) * 128:NT_OWN * 128, :],
                            in_=af_sb[:, (NT_OWN - 1) % 2, :])
                        marks[("a", "afs", NT_OWN - 1)] = cx.dmark(
                            f"gaf{(NT_OWN - 1) % 2}", inst)
                    blk.sync(g_afs_last)

            att_end = cx.snap()

        with ExitStack() as pf:
            w0s = sbt(pf, "w0s", [128, 2, FC, 512], BF16)
            dt = sbt(pf, "dt", [128, 64, 512], BF16)
            xng = sbt(pf, "xngf", [128, 2, FC, 512], BF16)
            w1s = sbt(pf, "w1s", [128, 4, 1024], BF16)
            ef_sb = sbt(pf, "ef_sb", [128, 4, 1024], F16)
            xo_sb = sbt(pf, "xo_sb", [128, 2, 1024], F32)
            at_sb = sbt(pf, "at_sb", [128, 2, 1024], F16)
            of_sb = sbt(pf, "of_sb", [128, 2, 1024], F32)
            b1_r = sbt(pf, "b1_r", [1, D], BF16)
            fps = pf.enter_context(nc.psum_tensor("fps", [128, 4, 1024], F32))

            def dview(r):
                return fps[:, r // 2, (r % 2) * 512:(r % 2) * 512 + 512]

            first = {e: True for e in ("pe", "dve", "act", "gq", "sq")}

            def fbar(name, h):
                if first[name]:
                    cx.barrier(name, h, att_end)
                    first[name] = False

            def seqkey(s):
                if s < 0:
                    return None
                return (s // 8, (s // 4) % 2, s % 4)

            if True:
                def g_b1(gp):
                    fbar("gq", gp)
                    marks["b1_done"] = cx.dmark("sb1", gp.dma_start(
                        out=b1_r[:], in_=b1))
                blk.gpsimd(g_b1)

                for g in range(4):
                    def s_xng(sy, g=g):
                        fbar("sq", sy)
                        W("sq", sy, "gxn0", marks[("q", "xnship", 3)])
                        W("sq", sy, "pe", marks.get(("p1g", g - 2)))
                        inst = sy.dma_start(
                            out=xng[:, g % 2].rearrange("p a b -> p (a b)"),
                            in_=xnt_d[g])
                        marks[("xng", g)] = cx.dmark(f"sxg{g % 2}", inst)
                    blk.sync(s_xng)

                    for nbg in range(16):
                        def s_w0t(sy, g=g, nbg=nbg):
                            if g == 0 and nbg == 0:
                                W("sq", sy, "gw0", marks["gw0_fin"])
                            if nbg >= 2:
                                W("sq", sy, "pe", marks.get(("p1", g, nbg - 2)))
                            elif nbg == 0:
                                W("sq", sy, "pe", marks.get(("p1", g - 1, 14)))
                            else:
                                W("sq", sy, "pe", marks.get(("p1", g - 1, 15)))
                            for fc in range(FC):
                                cx.dmark(f"sw0{nbg % 2}", sy.dma_start(
                                    out=w0s[:, nbg % 2, fc, :],
                                    in_=w0tb[fc * 128:(fc + 1) * 128,
                                             nbg * 512:(nbg + 1) * 512]))
                            marks[("w0t", g, nbg)] = cx.cnt[f"sw0{nbg % 2}"]
                        blk.sync(s_w0t)

                        def t_p1(te, g=g, nbg=nbg):
                            fbar("pe", te)
                            W("pe", te, f"sw0{nbg % 2}", marks[("w0t", g, nbg)])
                            W("pe", te, f"sxg{g % 2}", marks[("xng", g)])
                            for nc8 in range(4):
                                ncg = nbg * 4 + nc8
                                r = ncg % 8
                                if nbg < 2:
                                    W("pe", te, "act",
                                      marks.get(("eev", g - 1, 1, r // 2)))
                                W("pe", te, "act", marks.get(("gelu", g, ncg - 8)))
                                for fc in range(FC):
                                    inst = te.matmul(
                                        dview(r),
                                        w0s[:, nbg % 2, fc, nc8 * 128:(nc8 + 1) * 128],
                                        xng[:, g % 2, fc, :],
                                        start=(fc == 0), stop=(fc == FC - 1))
                                marks[("p1mm", g, ncg)] = M("pe", inst)
                            marks[("p1", g, nbg)] = cx.cnt["pe"]
                        blk.tensor(t_p1)

                        if nbg % 2 == 1:
                            def a_gelu(sc, g=g, nbg=nbg):
                                fbar("act", sc)
                                W("act", sc, "pe", marks.get(("p2last", g - 1)))
                                for ncg in range((nbg - 1) * 4, (nbg + 1) * 4):
                                    W("act", sc, "pe", marks[("p1mm", g, ncg)])
                                    marks[("gelu", g, ncg)] = M("act", sc.activation(
                                        dt[:, ncg, :], dview(ncg % 8), AF.Gelu,
                                        bias=b0t[:, ncg:ncg + 1]))
                            blk.scalar(a_gelu)

                    marks[("p1g", g)] = cx.cnt["pe"]

                    if debug and g == 0:
                        def g_ddbg(gp):
                            W("gq", gp, "act", marks[("gelu", 0, 63)])
                            cx.dmark("gdbg", gp.dma_start(
                                out=ddbg, in_=dt.ap().rearrange("p a b -> p (a b)")))
                        blk.gpsimd(g_ddbg)

                    for hh in range(2):
                        for nci in range(64):
                            def s_w1(sy, g=g, hh=hh, nci=nci):
                                W("sq", sy, "gw1", marks[("w1b", nci // 8)])
                                W("sq", sy, "pe",
                                  marks.get(("p2mm", g, hh, nci - 4)))
                                if nci < 4 and hh == 1:
                                    W("sq", sy, "pe",
                                      marks.get(("p2mm", g, 0, 60 + nci)))
                                elif nci < 4 and hh == 0:
                                    W("sq", sy, "pe",
                                      marks.get(("p2mm", g - 1, 1, 60 + nci)))
                                inst = sy.dma_start(
                                    out=w1s[:, nci % 4, :],
                                    in_=w1tb[nci * 128:(nci + 1) * 128,
                                             hh * 1024:(hh + 1) * 1024])
                                marks[("w1t", g, hh, nci)] = cx.dmark(
                                    f"sw1{nci % 4}", inst)
                            blk.sync(s_w1)

                            def t_p2(te, g=g, hh=hh, nci=nci):
                                W("pe", te, f"sw1{nci % 4}",
                                  marks[("w1t", g, hh, nci)])
                                if nci == 0:
                                    W("pe", te, "sb1", marks["b1_done"])
                                    W("pe", te, "act", marks[("gelu", g, 63)])
                                for tt in range(4):
                                    if nci == 0:
                                        W("pe", te, "act",
                                          marks.get(("eev", g, hh - 1, tt)))
                                        W("pe", te, "act",
                                          marks.get(("eev", g - 1, 1, tt)))
                                    te.matmul(fps[:, tt, 0:512],
                                              dt[:, nci, tt * 128:(tt + 1) * 128],
                                              w1s[:, nci % 4, 0:512],
                                              start=(nci == 0), stop=(nci == 63))
                                    inst = te.matmul(
                                        fps[:, tt, 512:1024],
                                        dt[:, nci, tt * 128:(tt + 1) * 128],
                                        w1s[:, nci % 4, 512:1024],
                                        start=(nci == 0), stop=(nci == 63))
                                    if nci == 0:
                                        te.matmul(
                                            fps[:, tt, 0:512], ones_r[:],
                                            b1_r[0:1, hh * 1024:hh * 1024 + 512],
                                            start=False, stop=False)
                                        inst = te.matmul(
                                            fps[:, tt, 512:1024], ones_r[:],
                                            b1_r[0:1,
                                                 hh * 1024 + 512:hh * 1024 + 1024],
                                            start=False, stop=False)
                                marks[("p2mm", g, hh, nci)] = M("pe", inst)
                            blk.tensor(t_p2)

                        marks[("p2last", g)] = cx.cnt["pe"]

                        def a_eev_all(sc, g=g, hh=hh):
                            W("act", sc, "pe", marks[("p2mm", g, hh, 63)])
                            for tt in range(4):
                                k = seqkey((g * 2 + hh) * 4 + tt - 8)
                                if k is not None:
                                    W("act", sc, "dve", marks.get(("fadd",) + k))
                                marks[("eev", g, hh, tt)] = M("act", sc.copy(
                                    ef_sb[:, tt, :], fps[:, tt, :]))
                        blk.scalar(a_eev_all)

                        for tt in range(4):
                            s = (g * 2 + hh) * 4 + tt

                            def s_ld(sc, g=g, hh=hh, tt=tt, s=s):
                                k = seqkey(s - 2)
                                if k is not None:
                                    W("act", sc, "dve", marks.get(("fadd",) + k))
                                r0 = g * 512 + tt * 128
                                cx.dmark(f"sxo{tt % 2}", sc.dma_start(
                                    out=xo_sb[:, tt % 2, :],
                                    in_=xo[r0:r0 + 128,
                                           hh * 1024:(hh + 1) * 1024]))
                                inst = sc.dma_start(
                                    out=at_sb[:, tt % 2, :],
                                    in_=att16[r0:r0 + 128,
                                              hh * 1024:(hh + 1) * 1024])
                                marks[("xoat", g, hh, tt)] = cx.dmark(
                                    f"sxo{tt % 2}", inst)
                            blk.scalar(s_ld)


                            if debug:
                                def g_edbg(gp, g=g, hh=hh, tt=tt):
                                    W("gq", gp, "act", marks[("eev", g, hh, tt)])
                                    r0 = g * 512 + tt * 128
                                    inst = gp.dma_start(
                                        out=edbg[r0:r0 + 128,
                                                 hh * 1024:(hh + 1) * 1024],
                                        in_=ef_sb[:, tt, :])
                                    marks[("edbg", g, hh, tt)] = cx.dmark(
                                        f"ged{tt % 2}", inst)
                                blk.gpsimd(g_edbg)

                            def v_fadd(ve, g=g, hh=hh, tt=tt, s=s):
                                fbar("dve", ve)
                                W("dve", ve, f"sxo{tt % 2}",
                                  marks[("xoat", g, hh, tt)])
                                W("dve", ve, "act", marks[("eev", g, hh, tt)])
                                k = seqkey(s - 2)
                                if k is not None:
                                    W("dve", ve, f"go{tt % 2}",
                                      marks.get(("ost",) + k))
                                m = M("dve", ve.tensor_add(
                                    of_sb[:, tt % 2, :], xo_sb[:, tt % 2, :],
                                    at_sb[:, tt % 2, :]))
                                W("dve", ve, "dve", m)
                                marks[("fadd", g, hh, tt)] = M("dve", ve.tensor_add(
                                    of_sb[:, tt % 2, :], of_sb[:, tt % 2, :],
                                    ef_sb[:, tt, :]))
                            blk.vector(v_fadd)

                            def g_ost(sc, g=g, hh=hh, tt=tt):
                                W("act", sc, "dve", marks[("fadd", g, hh, tt)])
                                r0 = g * 512 + tt * 128
                                inst = sc.dma_start(
                                    out=out[r0:r0 + 128,
                                            hh * 1024:(hh + 1) * 1024],
                                    in_=of_sb[:, tt % 2, :])
                                marks[("ost", g, hh, tt)] = cx.dmark(
                                    f"go{tt % 2}", inst)
                            blk.scalar(g_ost)

                def g_fin(gp):
                    for e in sorted(cx.sem):
                        if cx.cnt[e] > 0:
                            gp.wait_ge(cx.sem[e], cx.cnt[e])
                blk.gpsimd(g_fin)

    return nc


def make_inputs_for_core(c, inp):
    b, h = c // 2, c % 2
    x = np.asarray(inp["L0_attn_x"])
    xfull = np.ascontiguousarray(x[b])
    xown = np.ascontiguousarray(x[b].reshape(32, 128, D)[h::2].reshape(OWN, D))
    p = np.arange(128)[:, None]
    j = np.arange(256)[None, :]
    cmaskv = np.where(j > 128 * h + p, np.float32(NEG), np.float32(0.0))
    return {
        "x_full": xfull,
        "x_own": xown,
        "cmask": np.ascontiguousarray(cmaskv.astype(np.float32)),
        "wqt": np.ascontiguousarray(np.asarray(inp["L0_attn_WQ"]).T),
        "bq": np.asarray(inp["L0_attn_bQ"]).reshape(1, D),
        "wkt": np.ascontiguousarray(np.asarray(inp["L0_attn_WK"]).T),
        "bkt": np.ascontiguousarray(
            np.asarray(inp["L0_attn_bK"]).reshape(4, 128).T),
        "wvt": np.ascontiguousarray(np.asarray(inp["L0_attn_WV"]).T),
        "bv": np.asarray(inp["L0_attn_bV"]).reshape(1, GD),
        "proj": np.asarray(inp["L0_attn_proj"]),
        "w0t": np.ascontiguousarray(np.asarray(inp["L0_ffn_W0"]).T),
        "b0": np.ascontiguousarray(
            np.asarray(inp["L0_ffn_bias0"]).reshape(64, 128).T),
        "w1t": np.ascontiguousarray(np.asarray(inp["L0_ffn_W1"]).T),
        "b1": np.asarray(inp["L0_ffn_bias1"]).reshape(1, D),
    }


def kernel(**inputs):
    nc = build_program()
    in_maps = [make_inputs_for_core(c, inputs) for c in range(NCORES)]
    res = run_bass_kernel_spmd(nc, in_maps, core_ids=list(range(NCORES)))
    outa = np.empty((B, T, D), dtype=np.float32)
    for c in range(NCORES):
        b, h = c // 2, c % 2
        oc = np.asarray(res.results[c]["out_own"]).reshape(NT_OWN, 128, D)
        outa[b].reshape(32, 128, D)[h::2] = oc
    return outa
